# version 21
# baseline (speedup 1.0000x reference)
import contextlib
import numpy as np
import ml_dtypes
import concourse.bass as bass
import concourse.mybir as mybir
from concourse.bass_utils import run_bass_kernel_spmd

F32 = mybir.dt.float32
BF16 = mybir.dt.bfloat16
AF = mybir.ActivationFunctionType
ALU = mybir.AluOpType
AX = mybir.AxisListType.X
BFD = ml_dtypes.bfloat16

S = 2048
DMODEL = 1024
NT = 16
ND = 8
NP = 8
EXP = 8
CSHIFT = 12.0

ENGS = ("sync", "gpsimd", "tensor", "vector", "scalar")


class Sem:
    def __init__(self, nc, es, name, owner=None):
        self.h = es.enter_context(nc.semaphore(name))
        self.n = 0
        self.owner = owner


class Rec:
    def __init__(self):
        self.ops = {e: [] for e in ENGS}
        self.last_wait = {}

    def op(self, eng, fn, incs=()):
        incs = tuple(incs)
        main, extra = incs[:2], incs[2:]
        if extra:
            assert eng in ("vector", "scalar", "gpsimd"), "no engine_nop overflow on " + eng

        def f(e, fn=fn, main=main, extra=extra):
            i = fn(e)
            for s, k in main:
                i.then_inc(s.h, k)
            while extra:
                chunk, extra = extra[:2], extra[2:]
                j = e.engine_nop()
                for s, k in chunk:
                    j.then_inc(s.h, k)
        self.ops[eng].append(f)
        for s, k in incs:
            s.n += k

    def wait(self, eng, sem, val=None):
        if val is None:
            val = sem.n
        if val <= 0 or sem.owner == eng:
            return
        key = (eng, id(sem))
        if self.last_wait.get(key, -1) >= val:
            return
        self.last_wait[key] = val
        self.ops[eng].append(lambda e, s=sem, v=val: e.wait_ge(s.h, v))

    def raw(self, eng, fn):
        self.ops[eng].append(fn)

    def flush(self, nc):
        with nc.Block() as block:
            for name in ENGS:
                ops = self.ops[name]
                if ops:
                    getattr(block, name)(lambda e, ops=list(ops): [f(e) for f in ops])
        self.ops = {e: [] for e in ENGS}


class Rel:

    def __init__(self):
        self.evts = []
        self.acq = 0

    def acquire(self, r, eng):
        assert len(self.evts) >= self.acq, "acquire before prior release recorded"
        if self.acq > 0:
            sem, val = self.evts[self.acq - 1]
            r.wait(eng, sem, val)
        self.acq += 1

    def release(self, sem, val=None):
        self.evts.append((sem, val if val is not None else sem.n))


class Bank(Rel):
    def __init__(self, nc, es, idx):
        super().__init__()
        self.t = es.enter_context(nc.psum_tensor(f"pbank{idx}", [128, 512], F32))

    def acquire(self, r, eng="tensor"):
        super().acquire(r, eng)


class SubBank(Rel):
    def __init__(self, region, lo):
        super().__init__()
        self.t = region[:, lo:lo + 512]

    def acquire(self, r, eng="tensor"):
        super().acquire(r, eng)


class Buf(Rel):
    def __init__(self, nc, es, name, n, shape, dtype):
        super().__init__()
        self.ts = [es.enter_context(nc.sbuf_tensor(f"{name}{i}", shape, dtype))
                   for i in range(n)]
        self.nb = n

    def acquire(self, r, eng):
        u = self.acq
        if u >= self.nb:
            sem, val = self.evts[u - self.nb]
            r.wait(eng, sem, val)
        self.acq += 1
        return self.ts[u % self.nb]

    def reserve(self):
        u = self.acq
        w = self.evts[u - self.nb] if u >= self.nb else None
        self.acq += 1
        return self.ts[u % self.nb], w


def host_layouts(inp):
    f32 = np.float32
    a = {k: np.asarray(v, dtype=f32) for k, v in inp.items()}
    out = {}

    wq = a["L0_attn_w_qkv"].reshape(1024, 3072)
    out["wqkv_s"] = np.ascontiguousarray(
        wq.reshape(8, 128, 3, 8, 128).transpose(3, 1, 2, 0, 4)).reshape(8, 128, 3072)
    bq = a["L0_attn_b_qkv"].reshape(3072)
    out["bqkT"] = np.ascontiguousarray(bq.reshape(24, 128).T)
    out["brow_v"] = np.ascontiguousarray(bq[2048:].reshape(1, 1024))
    out["wout_s"] = np.ascontiguousarray(a["L0_attn_w_out"].reshape(1024, 1024))
    out["bout_s"] = np.ascontiguousarray(a["L0_attn_b_out"].reshape(8, 128).T)

    out["gate_ws"] = np.ascontiguousarray(
        a["L0_ffn_gate_w"].T.reshape(8, 128, 8).transpose(1, 0, 2).reshape(128, 64))
    out["gate_b"] = np.ascontiguousarray(np.broadcast_to(a["L0_ffn_gate_b"], (128, 8)))

    perm = np.concatenate([np.arange(0, 2048, 2), np.arange(1, 2048, 2)])

    def prep13(w):
        wp = w[..., perm, :]
        wt = np.swapaxes(wp, -1, -2)
        lead = wt.shape[:-2]
        t = wt.reshape(lead + (8, 128, 16, 128))
        t = np.moveaxis(t, -2, -4)
        t = np.swapaxes(t, -3, -2)
        return t

    def w13_fused(w1, w3):
        t1, t3 = prep13(w1), prep13(w3)
        A1, B1 = t1[..., 0:8, :, :, :], t1[..., 8:16, :, :, :]
        A3, B3 = t3[..., 0:8, :, :, :], t3[..., 8:16, :, :, :]
        st = np.stack([A1, A3, B1, B3], axis=-3)
        lead = st.shape[:-4]
        return np.ascontiguousarray(st.reshape(lead + (128, 4096)))

    out["w13_s"] = w13_fused(a["L0_ffn_w1"], a["L0_ffn_w3"])
    out["sw13_s"] = w13_fused(a["L0_ffn_sw1"][None], a["L0_ffn_sw3"][None])[0]

    def bperm(b):
        bp = b[..., perm]
        lead = bp.shape[:-1]
        t = bp.reshape(lead + (16, 128))
        t = np.moveaxis(t, -1, 0)
        return np.ascontiguousarray(t.reshape((128, -1)))
    out["b1_s"] = bperm(a["L0_ffn_b1"])
    out["b3_s"] = bperm(a["L0_ffn_b3"])
    out["sb1_s"] = bperm(a["L0_ffn_sb1"])
    out["sb3_s"] = bperm(a["L0_ffn_sb3"])

    def w2_slab(w2):
        wt = np.swapaxes(w2, -1, -2)
        lead = wt.shape[:-2]
        t = wt.reshape(lead + (8, 128, 1024)).swapaxes(-3, -2)
        return np.ascontiguousarray(t.reshape(lead + (128, 8192)))
    out["w2_s"] = w2_slab(a["L0_ffn_w2"])
    out["sw2_s"] = w2_slab(a["L0_ffn_sw2"][None])[0]
    out["b2_s"] = np.ascontiguousarray(a["L0_ffn_b2"])
    out["sb2_s"] = np.ascontiguousarray(a["L0_ffn_sb2"].reshape(8, 128).T)
    return out


def build_nc():
    nc = bass.Bass("TRN2", target_bir_lowering=False)
    r = Rec()

    d = {}
    for k, shape in {
        "x": [S, DMODEL], "bqkT": [128, 24], "bout_s": [128, 8], "sb2_s": [128, 8],
        "gate_b": [128, 8], "b1_s": [128, 128], "b3_s": [128, 128],
        "sb1_s": [128, 16], "sb3_s": [128, 16],
        "wqkv_s": [8, 128, 3072], "brow_v": [1, 1024], "wout_s": [1024, 1024],
        "gate_ws": [128, 64], "w13_s": [EXP, 8, 128, 4096], "w2_s": [EXP, 128, 8192],
        "sw13_s": [8, 128, 4096], "sw2_s": [128, 8192], "b2_s": [8, 1024],
    }.items():
        d[k] = nc.dram_tensor(k, shape, F32, kind="ExternalInput").ap()
    out_hbm = nc.dram_tensor("out", [S, DMODEL], F32, kind="ExternalOutput").ap()

    tri_np = np.triu(np.ones((128, 128), dtype=BFD))
    amask_np = ((1.0 - np.triu(np.ones((128, 128)))) * -3e9).astype(BFD)
    identb_np = np.eye(128, dtype=BFD)
    identf_np = np.eye(128, dtype=np.float32)
    sel_np = np.zeros((8, 8 * 128), dtype=BFD)
    for e in range(8):
        sel_np[e, e * 128:(e + 1) * 128] = 1
    sel2_np = np.zeros((33, 128), dtype=BFD)
    sel2_np[0, 0:64] = 1
    sel2_np[32, 64:128] = 1
    d_sel2 = nc.inline_tensor(sel2_np, name="sel2_c").ap()
    d_tri = nc.inline_tensor(tri_np, name="tri_c").ap()
    d_amask = nc.inline_tensor(amask_np, name="amask_c").ap()
    d_idb = nc.inline_tensor(identb_np, name="identb_c").ap()
    d_idf = nc.inline_tensor(identf_np, name="identf_c").ap()
    d_sel = nc.inline_tensor(sel_np, name="sel_c").ap()

    es0 = contextlib.ExitStack()
    sb0 = lambda name, shape, dtype: es0.enter_context(nc.sbuf_tensor(name, shape, dtype))

    xnf = sb0("xnf", [128, ND * S], BF16)
    acc = sb0("acc", [128, ND * S], F32)
    bqkT = sb0("bqkTs", [128, 24], F32)
    bout = sb0("bouts", [128, 8], F32)
    sb2c = sb0("sb2cs", [128, 8], F32)
    binit = sb0("binit", [128, 8], F32)
    negC = sb0("negC", [128, 1], F32)
    cInv = sb0("cInv", [128, 1], F32)
    identf = sb0("identf", [128, 128], F32)
    identb = sb0("identb", [128, 128], BF16)
    tri = sb0("tri", [128, 128], BF16)
    amask = sb0("amask", [128, 128], BF16)
    ones_bf = sb0("ones_bf", [1, 128], BF16)
    brow_v = sb0("brow_vs", [1, 1024], BF16)

    sx = Sem(nc, es0, "sx", "sync")
    gdc = Sem(nc, es0, "gdc", "gpsimd")
    gdw = Sem(nc, es0, "gdw", "gpsimd")
    gdf = Sem(nc, es0, "gdf", "gpsimd")
    gdg = Sem(nc, es0, "gdg", "gpsimd")
    wqs = [Sem(nc, es0, f"wqs{i}", "gpsimd") for i in range(6)]
    w13s = [Sem(nc, es0, f"w13s{i}", "gpsimd") for i in range(12)]
    w2s = Sem(nc, es0, "w2s", "gpsimd")
    ods = [Sem(nc, es0, f"ods{i}", "sync") for i in range(2)]
    sxs_ = [Sem(nc, es0, f"sxc{i}", "sync") for i in range(4)]
    md = Sem(nc, es0, "md", "tensor")
    vd = Sem(nc, es0, "vd", "vector")
    sd = Sem(nc, es0, "sd", "scalar")

    def dma(q, dst, src, sem):
        r.op(q, lambda e, dst=dst, src=src: e.dma_start(dst, src), [(sem, 16)])
        return sem.n

    for nm, t in [("bqkT", bqkT), ("bout_s", bout), ("sb2_s", sb2c)]:
        dma("gpsimd", t[:, :], d[nm], gdc)
    dma("gpsimd", brow_v[0:1, :], d["brow_v"], gdc)
    dma("gpsimd", tri[:, :], d_tri, gdc)
    dma("gpsimd", amask[:, :], d_amask, gdc)
    dma("gpsimd", identf[:, :], d_idf, gdc)
    dma("gpsimd", identb[:, :], d_idb, gdc)
    gd_consts = gdc.n

    r.op("vector", lambda e: e.memset(ones_bf[0:1, :], 1.0))
    r.op("vector", lambda e: e.memset(negC[:, :], -float(CSHIFT)))
    r.op("vector", lambda e: e.memset(cInv[:, :], 1.0 / 1.702), [(vd, 1)])
    vd_consts = vd.n
    r.wait("vector", gdc, gd_consts)
    r.op("vector", lambda e: e.tensor_add(binit[:, :], bout[:, :], sb2c[:, :]), [(vd, 1)])
    vd_binit = vd.n

    es1 = contextlib.ExitStack()
    sb1_ = lambda name, shape, dtype: es1.enter_context(nc.sbuf_tensor(name, shape, dtype))
    x_tm = sb1_("x_tm", [128, NT * 1024], F32)
    xn_tm = Buf(nc, es1, "xn_tm", 2, [128, 1024], BF16)
    bst = sb1_("bst", [128, NT * 12], F32)
    mv = sb1_("mv", [128, NT * 2], F32)
    stv = sb1_("stv", [128, 16], F32)
    str_ = sb1_("str", [128, 16], F32)
    stn = sb1_("stn", [128, 16], F32)
    pb_t = [es1.enter_context(nc.psum_tensor(f"pbf{i}", [128, 1024], BF16)) for i in range(2)]
    pf_t = [es1.enter_context(nc.psum_tensor(f"pff{i}", [128, 512], F32)) for i in range(4)]
    pbR = [Rel(), Rel()]
    pfR = [Rel(), Rel(), Rel(), Rel()]

    for ch in range(4):
        dma("sync",
            x_tm[:, ch * 4096:(ch + 1) * 4096].rearrange("p (t c) -> p t c", t=4),
            d["x"][ch * 512:(ch + 1) * 512, :].rearrange("(t p) c -> p t c", p=128),
            sxs_[ch])
    for ti in range(NT):
        r.wait("vector", sxs_[ti // 4], 16)
        r.op("vector", lambda e, ti=ti: e.bn_stats(
            bst[:, ti * 12: ti * 12 + 6], x_tm[:, ti * 1024: ti * 1024 + 512]))
        r.op("vector", lambda e, ti=ti: e.bn_stats(
            bst[:, ti * 12 + 6: ti * 12 + 12], x_tm[:, ti * 1024 + 512: (ti + 1) * 1024]))
    r.raw("vector", lambda e: e.drain())
    for ti in range(NT):
        r.op("vector", lambda e, ti=ti: e.bn_aggr(
            mv[:, ti * 2: ti * 2 + 2], bst[:, ti * 12: (ti + 1) * 12]))
    r.raw("vector", lambda e: e.drain())
    mv3 = mv[:, :].rearrange("p (t two) -> p t two", two=2)
    r.op("vector", lambda e: e.tensor_scalar(
        stv[:, :].rearrange("p (t one) -> p t one", one=1), mv3[:, :, 1:2],
        1e-5, None, ALU.add), [(vd, 1)])
    r.wait("scalar", vd, vd.n)
    r.op("scalar", lambda e: e.activation(stv[:, :], stv[:, :], AF.Sqrt), [(sd, 1)])
    r.wait("vector", sd, sd.n)
    r.op("vector", lambda e: e.reciprocal(str_[:, :], stv[:, :]))
    r.raw("vector", lambda e: e.drain())
    r.op("vector", lambda e: e.scalar_tensor_tensor(
        stn[:, :].rearrange("p (t one) -> p t one", one=1), mv3[:, :, 0:1], -1.0,
        str_[:, :].rearrange("p (t one) -> p t one", one=1),
        ALU.mult, ALU.mult), [(vd, 1)])
    r.raw("vector", lambda e: e.drain())
    vd_stats = vd.n

    vd_xn_at = [0] * NT
    for ti in range(NT):
        xb = xn_tm.acquire(r, "vector")
        r.wait("vector", vd, vd_stats)
        r.op("vector", lambda e, ti=ti, xb=xb: e.tensor_scalar(
            xb[:, :], x_tm[:, ti * 1024:(ti + 1) * 1024],
            str_[:, ti:ti + 1], stn[:, ti:ti + 1], ALU.mult, ALU.add), [(vd, 1)])
        r.wait("tensor", vd, vd.n)
        pb = pb_t[ti % 2]
        pbR[ti % 2].acquire(r, "tensor")
        r.wait("tensor", gdc, gd_consts)
        for dk in range(ND):
            incs = [(md, 1)] if dk == ND - 1 else []
            r.op("tensor", lambda e, dk=dk, xb=xb, pb=pb: e.transpose(
                pb[:, dk * 128:(dk + 1) * 128], xb[:, dk * 128:(dk + 1) * 128],
                identb[:, :]), incs)
        md_xn = md.n
        xn_tm.release(md)
        pfb = (ti % 2) * 2
        pfR[pfb].acquire(r, "tensor")
        pfR[pfb + 1].acquire(r, "tensor")
        for dk in range(ND):
            incs = [(md, 1)] if dk % 4 == 3 else []
            r.op("tensor", lambda e, dk=dk, ti=ti, pfb=pfb: e.transpose(
                pf_t[pfb + dk // 4][:, (dk % 4) * 128:(dk % 4 + 1) * 128],
                x_tm[:, ti * 1024 + dk * 128: ti * 1024 + (dk + 1) * 128],
                identf[:, :]), incs)
        md_x = md.n
        r.wait("vector", md, md_xn)
        r.op("vector", lambda e, ti=ti, pb=pb: e.tensor_copy(
            xnf[:, :].rearrange("p (dk s) -> p dk s", dk=ND)[:, :, ti * 128:(ti + 1) * 128],
            pb[:, :].rearrange("p (dk c) -> p dk c", dk=ND)), [(vd, 1)])
        vd_xn_at[ti] = vd.n
        pbR[ti % 2].release(vd)
        r.wait("scalar", md, md_x)
        r.wait("scalar", vd, vd_binit)
        for dk in range(ND):
            incs = [(sd, 1)] if dk == 3 or dk == 7 else []
            r.op("scalar", lambda e, dk=dk, ti=ti, pfb=pfb: e.activation(
                acc[:, dk * S + ti * 128: dk * S + (ti + 1) * 128],
                pf_t[pfb + dk // 4][:, (dk % 4) * 128:(dk % 4 + 1) * 128], AF.Identity,
                bias=binit[:, dk:dk + 1]), incs)
            if dk == 3:
                pfR[pfb].release(sd)
            if dk == 7:
                pfR[pfb + 1].release(sd)
    vd_xn_done = vd.n
    sd_acc_init = sd.n
    r.flush(nc)
    es1.close()

    esP = contextlib.ExitStack()
    banks = [Bank(nc, esP, i) for i in range(4)]
    reg45 = esP.enter_context(nc.psum_tensor("reg45", [128, 1024], F32))
    reg67 = esP.enter_context(nc.psum_tensor("reg67", [128, 1024], F32))
    banks = banks + [SubBank(reg45, 0), SubBank(reg45, 512),
                     SubBank(reg67, 0), SubBank(reg67, 512)]
    regions = [(reg45, banks[4], banks[5]), (reg67, banks[6], banks[7])]
    rot_i = [0]

    def rot_next():
        reg = regions[rot_i[0] % 2]
        rot_i[0] += 1
        reg[1].acquire(r)
        reg[2].acquire(r)
        return reg

    def rot_release(reg, sem):
        reg[1].release(sem)
        reg[2].release(sem)

    es2 = contextlib.ExitStack()
    wqkvb = Buf(nc, es2, "wqkvb", 2, [128, 3072], BF16)
    qkb = Buf(nc, es2, "qkb", 2, [128, 4096], BF16)
    vhb = Buf(nc, es2, "vhb", 2, [128, 16 * 160], BF16)
    ptb = Buf(nc, es2, "ptb", 4, [128, 1024], BF16)
    lvf = Buf(nc, es2, "lvf", 2, [128, 512], F32)
    lvb = Buf(nc, es2, "lvb", 2, [64, 512], BF16)
    lvc = Buf(nc, es2, "lvc", 2, [128, 512], F32)
    ansT = es2.enter_context(nc.sbuf_tensor("ansT", [128, 8 * 2048], BF16))
    woutb = es2.enter_context(nc.sbuf_tensor("woutb", [128, 8192], BF16))
    sel2 = es2.enter_context(nc.sbuf_tensor("sel2", [33, 128], BF16))
    gd_sel2 = dma("gpsimd", sel2[0:33, :], d_sel2, gdc)
    for i in range(2):
        r.op("vector", lambda e, i=i: e.memset(lvf.ts[i][0:64, :], 1.0))
    r.raw("vector", lambda e: e.drain())

    for i in range(2):
        r.op("vector", lambda e, i=i: e.memset(
            vhb.ts[i][:, :].rearrange("p (kt z) -> p kt z", z=160)[:, :, 64:65], 1.0))
        r.op("vector", lambda e, i=i: e.memset(
            vhb.ts[i][:, :].rearrange("p (kt z) -> p kt z", z=160)[:, :, 144:145], 1.0),
            [(vd, 1)])
    vd_vones = vd.n
    gd_wout_box = [0]

    pair_sd_done = {}

    def make_qkv_jobs(p):
        wt = wqkvb.acquire(r, "gpsimd")
        qb = (p % 2) * 3
        w_ats = [dma("gpsimd", wt[:, part * 1024:(part + 1) * 1024],
                     d["wqkv_s"][p][:, part * 1024:(part + 1) * 1024],
                     wqs[qb + part]) for part in range(3)]
        qt, qk_w = qkb.reserve()
        vt, vh_w = vhb.reserve()
        jobs = []

        def qk_job(tq, qc, first):
            def f():
                if qc == 0:
                    r.wait("tensor", wqs[qb + tq], w_ats[tq])
                if first:
                    if qk_w:
                        r.wait("scalar", *qk_w)
                r.wait("tensor", vd, vd_xn_at[qc * 4 + 3])
                rr = rot_next()
                for dk in range(ND):
                    incs = [(md, 1)] if dk == ND - 1 else []
                    r.op("tensor", lambda e, tq=tq, dk=dk, qc=qc, rr=rr, wt=wt: e.matmul(
                        rr[0][:, 0:512], wt[:, (tq * 8 + dk) * 128:(tq * 8 + dk + 1) * 128],
                        xnf[:, dk * S + qc * 512: dk * S + qc * 512 + 512],
                        start=(dk == 0), stop=(dk == ND - 1)), incs)
                r.wait("scalar", md, md.n)
                r.op("scalar", lambda e, tq=tq, qc=qc, rr=rr, qt=qt, p=p: e.activation(
                    qt[:, tq * 2048 + qc * 512: tq * 2048 + (qc + 1) * 512], rr[0][:, 0:512],
                    AF.Identity, bias=bqkT[:, tq * 8 + p: tq * 8 + p + 1]), [(sd, 1)])
                rot_release(rr, sd)
            return f

        def v_job(kt, first):
            def f():
                if first:
                    r.wait("tensor", wqs[qb + 2], w_ats[2])
                    r.wait("tensor", vd, vd_vones)
                    if vh_w:
                        r.wait("scalar", *vh_w)
                r.wait("tensor", vd, vd_xn_at[kt])
                rr = rot_next()
                for dk in range(ND):
                    r.op("tensor", lambda e, kt=kt, dk=dk, rr=rr, wt=wt: e.matmul(
                        rr[0][:, 0:128], xnf[:, dk * S + kt * 128: dk * S + (kt + 1) * 128],
                        wt[:, (16 + dk) * 128:(16 + dk + 1) * 128],
                        start=(dk == 0), stop=False))
                r.op("tensor", lambda e, rr=rr, p=p: e.matmul(
                    rr[0][:, 0:128], ones_bf[0:1, 0:128], brow_v[0:1, p * 128:(p + 1) * 128],
                    start=False, stop=True), [(md, 1)])
                r.wait("scalar", md, md.n)
                r.op("scalar", lambda e, kt=kt, rr=rr, vt=vt: e.copy(
                    vt[:, kt * 160:kt * 160 + 160].rearrange("p (h z) -> p h z", z=80)[:, :, 0:64],
                    rr[0][:, 0:128].rearrange("p (h z) -> p h z", z=64)), [(sd, 1)])
                rot_release(rr, sd)
                if kt == NT - 1:
                    wqkvb.release(sd)
                    pair_sd_done[p] = sd.n
            return f

        for tq in (0, 1):
            for qc in range(4):
                jobs.append(qk_job(tq, qc, tq == 0 and qc == 0))
        for kt in range(NT):
            jobs.append(v_job(kt, kt == 0))
        return jobs, qt, vt

    def emit_attn(p, qt, vt, pump):
        r.wait("tensor", sd, pair_sd_done[p])
        for half in (0, 1):
            q0 = half * 1024
            nki = 8 if half == 0 else 16
            exp_at = [[0] * nki for _ in range(2)]
            pt_of = [[None] * nki for _ in range(2)]

            def emit_scores(hl, ki, half=half, q0=q0):
                off = hl * 64
                pt = ptb.acquire(r, "scalar")
                pt_of[hl][ki] = pt
                rr = rot_next()
                reg = rr[0]
                qs0 = max(ki * 128, q0)
                have_diag = ki * 128 >= q0
                diag_hb = (ki * 128 - q0) // 512 if have_diag else -1
                for hb in (0, 1):
                    lo = max(qs0, q0 + 512 * hb)
                    hi = q0 + 512 * (hb + 1)
                    if lo >= hi:
                        continue
                    din = diag_hb == hb
                    lastb = hb == 1 or (hb == 0 and qs0 >= q0 + 512)
                    r.op("tensor", lambda e, ki=ki, lo=lo, n=hi - lo, off=off, q0=q0,
                         reg=reg, din=din: e.matmul(
                        reg[:, lo - q0: lo - q0 + n],
                        qt[off:off + 64, 2048 + ki * 128: 2048 + (ki + 1) * 128],
                        qt[off:off + 64, lo:lo + n], start=True, stop=not din),
                        [(md, 1)] if (not din and hb == 1) else [])
                    if din:
                        r.op("tensor", lambda e, ki=ki, q0=q0, reg=reg: e.matmul(
                            reg[:, ki * 128 - q0: ki * 128 - q0 + 128], identb[:, :],
                            amask[:, :], start=False, stop=True),
                            [(md, 1)] if hb == 1 else [])
                if (diag_hb == 0 and qs0 < q0 + 512) or False:
                    pass
                last_sc = hl == 1 and half == 1 and ki == NT - 1
                if last_sc:
                    qkb.release(md)
                r.wait("scalar", md, md.n)
                r.op("scalar", lambda e, qs0=qs0, q0=q0, pt=pt, reg=reg: e.activation(
                    pt[:, qs0 - q0: 1024], reg[:, qs0 - q0: 1024], AF.Exp, scale=0.125,
                    bias=negC[:, 0:1]), [(sd, 1)])
                rot_release(rr, sd)
                exp_at[hl][ki] = sd.n

            def emit_av(hl, ki, half=half, q0=q0):
                off = hl * 64
                slot = hl * 80
                if ki == 0:
                    while evac_pend:
                        evac_pend.pop(0)[1]()
                    for lqc in range(2):
                        banks[hl * 2 + lqc].acquire(r)
                pt = pt_of[hl][ki]
                r.wait("tensor", sd, exp_at[hl][ki])
                live = [lqc for lqc in range(2)
                        if ki * 128 < (half * 2 + lqc + 1) * 512]
                for lqc in live:
                    qc = half * 2 + lqc
                    cs = max(0, ki * 128 - qc * 512)
                    n = 512 - cs
                    last_of_qc = ki == min(NT - 1, 4 * qc + 3)
                    incs = [(md, 1)] if lqc == live[-1] else []
                    r.op("tensor", lambda e, hl=hl, lqc=lqc, qc=qc, ki=ki, cs=cs, n=n,
                         pt=pt, slot=slot, q0=q0, last=last_of_qc: e.matmul(
                        banks[hl * 2 + lqc].t[0:65, cs:cs + n],
                        vt[:, ki * 160 + slot: ki * 160 + slot + 65],
                        pt[:, qc * 512 + cs - q0: qc * 512 + cs - q0 + n],
                        start=(ki == 0), stop=last), incs)
                ptb.release(md)

            def emit_evac(lqc, half=half):
                qc = half * 2 + lqc
                bk0 = banks[lqc]
                bk1 = banks[2 + lqc]
                lf = lvf.acquire(r, "vector")
                lb = lvb.acquire(r, "vector")
                r.wait("vector", md, md.n)
                r.op("vector", lambda e, bk0=bk0, lf=lf: e.tensor_copy(
                    lf[0:1, :], bk0.t[64:65, 0:512]))
                r.op("vector", lambda e, bk1=bk1, lf=lf: e.tensor_copy(
                    lf[32:33, :], bk1.t[64:65, 0:512]))
                r.raw("vector", lambda e: e.drain())

                def _recip(e, lf=lf, lb=lb):
                    with nc.allow_low_precision(reason="bf16 store of recip"):
                        e.reciprocal(lb[0:64, :], lf[0:64, :])
                r.op("vector", _recip)
                r.raw("vector", lambda e: e.drain())
                r.op("vector", lambda e: e.engine_nop(), [(vd, 1)])
                lvf.release(vd)
                vd_lb = vd.n

                def rest(qc=qc, lb=lb, vd_lb=vd_lb, p=p, bk0=bk0, bk1=bk1):
                    rr = rot_next()
                    r.wait("tensor", gdc, gd_sel2)
                    r.wait("tensor", vd, vd_lb)
                    r.op("tensor", lambda e, lb=lb, rr=rr: e.matmul(
                        rr[0][:, 0:512], sel2[0:33, :], lb[0:33, 0:512],
                        start=True, stop=True), [(md, 1)])
                    lvb.release(md)
                    r.wait("vector", md, md.n)
                    lc = lvc.acquire(r, "vector")
                    r.op("vector", lambda e, rr=rr, lc=lc: e.tensor_copy(
                        lc[:, :], rr[0][:, 0:512]), [(vd, 1)])
                    rot_release(rr, vd)
                    r.raw("vector", lambda e: e.drain())
                    r.op("vector", lambda e, qc=qc, lc=lc, p=p, bk0=bk0: e.tensor_mul(
                        ansT[0:64, p * 2048 + qc * 512: p * 2048 + (qc + 1) * 512],
                        bk0.t[0:64, 0:512], lc[0:64, :]), [(vd, 1)])
                    bk0.release(vd)
                    r.op("vector", lambda e, qc=qc, lc=lc, p=p, bk1=bk1: e.tensor_mul(
                        ansT[64:128, p * 2048 + qc * 512: p * 2048 + (qc + 1) * 512],
                        bk1.t[0:64, 0:512], lc[64:128, :]), [(vd, 1)])
                    bk1.release(vd)
                    lvc.release(vd)
                evac_pend.append((step_box[0], rest))

            for hl in (0, 1):
                emit_scores(hl, 0)
            if half == 1:
                pump()
                pump()
            for ki in range(nki):
                step_box[0] += 1
                if ki + 1 < nki:
                    emit_scores(0, ki + 1)
                emit_av(0, ki)
                if ki + 1 < nki:
                    emit_scores(1, ki + 1)
                emit_av(1, ki)
                if half == 1 and ki == NT - 1:
                    vhb.release(md)
                lqc_done = None
                if ki == 4 * (half * 2) + 3:
                    lqc_done = 0
                elif ki == min(NT - 1, 4 * (half * 2 + 1) + 3):
                    lqc_done = 1
                if lqc_done is not None:
                    emit_evac(lqc_done)
                pump()
                if (lqc_done is None and evac_pend
                        and step_box[0] - evac_pend[0][0] >= 2):
                    evac_pend.pop(0)[1]()

    evac_pend = []
    step_box = [0]
    jobs0, qt0, vt0 = make_qkv_jobs(0)
    gd_wout_box[0] = dma(
        "gpsimd", woutb[:, :].rearrange("p (hek dm) -> p hek dm", hek=8),
        d["wout_s"].rearrange("(hek p) dm -> p hek dm", p=128), gdw)
    for jb in jobs0:
        jb()
    prev = (qt0, vt0)
    pending = []

    def pump():
        if pending:
            pending.pop(0)()

    for p in range(NP):
        if p + 1 < NP:
            jobs, qtn, vtn = make_qkv_jobs(p + 1)
            pending.extend(jobs)
            nxt = (qtn, vtn)
        else:
            nxt = None
        emit_attn(p, prev[0], prev[1], pump)
        while pending:
            pending.pop(0)()
        prev = nxt

    while evac_pend:
        evac_pend.pop(0)[1]()
    r.wait("tensor", gdw, gd_wout_box[0])
    r.wait("tensor", vd, vd.n)
    for dm in range(ND):
        for qc in range(4):
            b = banks[(dm % 2) * 4 + qc]
            b.acquire(r)
            for hek in range(8):
                incs = [(md, 1)] if hek == 7 else []
                r.op("tensor", lambda e, dm=dm, qc=qc, hek=hek, b=b: e.matmul(
                    b.t[:, :], woutb[:, hek * 1024 + dm * 128: hek * 1024 + (dm + 1) * 128],
                    ansT[:, hek * 2048 + qc * 512: hek * 2048 + (qc + 1) * 512],
                    start=(hek == 0), stop=(hek == 7)), incs)
            r.wait("vector", md, md.n)
            r.wait("vector", sd, sd_acc_init)
            r.op("vector", lambda e, dm=dm, qc=qc, b=b: e.tensor_add(
                acc[:, dm * S + qc * 512: dm * S + (qc + 1) * 512],
                acc[:, dm * S + qc * 512: dm * S + (qc + 1) * 512], b.t[:, :]),
                [(vd, 1)])
            b.release(vd)
    r.flush(nc)
    es2.close()

    es3 = contextlib.ExitStack()
    sb3 = lambda name, shape, dtype: es3.enter_context(nc.sbuf_tensor(name, shape, dtype))
    gatb = sb3("gatbs", [128, 8], F32)
    b1c = sb3("b1cs", [128, 128], F32)
    b3c = sb3("b3cs", [128, 128], F32)
    sb1c = sb3("sb1cs", [128, 16], F32)
    sb3c = sb3("sb3cs", [128, 16], F32)
    selm = sb3("selm", [8, 1024], BF16)
    b2c = sb3("b2cs", [8, 1024], BF16)
    gate_ws = sb3("gate_wss", [128, 64], BF16)
    cw_T = sb3("cw_Ts", [8, 2048], BF16)
    w13b = Buf(nc, es3, "w13b", 3, [128, 4096], BF16)
    w2b = Buf(nc, es3, "w2b", 1, [128, 8192], BF16)
    gd_gate_w = dma("gpsimd", gate_ws[:, :], d["gate_ws"], gdg)
    dma("gpsimd", gatb[:, :], d["gate_b"], gdf)
    wsl0 = w13b.acquire(r, "gpsimd")
    j0_ats = [dma("gpsimd", wsl0[:, 0:1024], d["sw13_s"][0][:, 0:1024], w13s[0])]
    for nm, t in [("b1_s", b1c), ("b3_s", b3c),
                  ("sb1_s", sb1c), ("sb3_s", sb3c)]:
        dma("gpsimd", t[:, :], d[nm], gdf)
    for part in range(1, 4):
        j0_ats.append(dma("gpsimd", wsl0[:, part * 1024:(part + 1) * 1024],
                          d["sw13_s"][0][:, part * 1024:(part + 1) * 1024],
                          w13s[part]))
    staged_j0 = (wsl0, j0_ats, 0)
    dma("gpsimd", b2c[0:8, :], d["b2_s"], gdf)
    dma("gpsimd", selm[0:8, :], d_sel, gdf)
    gd_fconsts = gdf.n
    r.wait("tensor", gdg, gd_gate_w)
    r.wait("vector", gdf, gd_fconsts)
    lg_sb = sb3("lg_sb", [8, 512], F32)
    lg_tm = sb3("lg_tm", [128, 128], F32)
    em = sb3("em", [128, 128], F32)
    sct = sb3("sct", [128, 128], F32)
    mk1 = sb3("mk1", [128, 128], F32)
    msk = sb3("msk", [128, 128], F32)
    mk2 = sb3("mk2", [128, 128], F32)
    cwa = sb3("cwa", [128, 128], F32)
    gs1 = sb3("gs1", [128, 16], F32)
    gs2 = sb3("gs2", [128, 16], F32)
    gm1 = sb3("gm1", [128, 16], F32)
    gm2 = sb3("gm2", [128, 16], F32)
    a_sb = sb3("a_sb", [128, 8 * 2048], BF16)
    cwb = sb3("cwb", [128, 2048], BF16)
    pa = [sb3(f"pa{i}", [128, 512], F32) for i in range(2)]
    pbx = [sb3(f"pbx{i}", [128, 512], F32) for i in range(2)]
    mm_ = [sb3(f"mm{i}", [128, 512], F32) for i in range(2)]
    lp_ = [sb3(f"lp{i}", [128, 512], F32) for i in range(2)]
    aa_s = sb3("aa_s", [128, 512], BF16)
    b6 = banks[6]
    b6.acquire(r)
    for qc in range(4):
        b = banks[4 + qc % 2]
        b.acquire(r)
        for dk in range(ND):
            incs = [(md, 1)] if dk == ND - 1 else []
            r.op("tensor", lambda e, dk=dk, qc=qc, b=b: e.matmul(
                b.t[0:8, :], gate_ws[:, dk * 8:(dk + 1) * 8],
                xnf[:, dk * S + qc * 512: dk * S + qc * 512 + 512],
                start=(dk == 0), stop=(dk == ND - 1)), incs)
        r.wait("vector", md, md.n)
        r.op("vector", lambda e, b=b: e.tensor_copy(
            lg_sb[0:8, 0:512], b.t[0:8, :]), [(vd, 1)])
        b.release(vd)
        r.wait("tensor", vd, vd.n)
        for k in range(4):
            ti = qc * 4 + k
            incs = [(md, 1)] if k == 3 else []
            r.op("tensor", lambda e, ti=ti, k=k: e.transpose(
                b6.t[:, ti * 8:(ti + 1) * 8], lg_sb[0:8, k * 128:(k + 1) * 128],
                identf[0:8, 0:8]), incs)
        r.wait("vector", md, md.n)
    r.op("vector", lambda e: e.tensor_copy(lg_tm[:, 0:128], b6.t[:, 0:128]),
         [(vd, 1)])
    b6.release(vd)
    r.wait("scalar", vd, vd.n)
    r.op("scalar", lambda e: e.activation(em[:, :], lg_tm[:, :], AF.Exp), [(sd, 1)])
    r.wait("vector", sd, sd.n)
    for ti in range(NT):
        r.op("vector", lambda e, ti=ti: e.reduce_sum(
            gs1[:, ti:ti + 1], em[:, ti * 8:(ti + 1) * 8], axis=AX))
    r.raw("vector", lambda e: e.drain())
    r.op("vector", lambda e: e.reciprocal(gs2[:, :], gs1[:, :]))
    r.raw("vector", lambda e: e.drain())
    for ti in range(NT):
        r.op("vector", lambda e, ti=ti: e.tensor_scalar(
            sct[:, ti * 8:(ti + 1) * 8], em[:, ti * 8:(ti + 1) * 8],
            gs2[:, ti:ti + 1], None, ALU.mult))
    r.raw("vector", lambda e: e.drain())
    bia = lg_tm
    for ti in range(NT):
        r.op("vector", lambda e, ti=ti: e.tensor_add(
            bia[:, ti * 8:(ti + 1) * 8], sct[:, ti * 8:(ti + 1) * 8], gatb[:, :]))
    r.raw("vector", lambda e: e.drain())
    for ti in range(NT):
        r.op("vector", lambda e, ti=ti: e.reduce_max(
            gm1[:, ti:ti + 1], bia[:, ti * 8:(ti + 1) * 8], axis=AX))
    r.raw("vector", lambda e: e.drain())
    for ti in range(NT):
        r.op("vector", lambda e, ti=ti: e.tensor_scalar(
            mk1[:, ti * 8:(ti + 1) * 8], bia[:, ti * 8:(ti + 1) * 8],
            gm1[:, ti:ti + 1], None, ALU.is_ge))
    r.raw("vector", lambda e: e.drain())
    for ti in range(NT):
        r.op("vector", lambda e, ti=ti: e.scalar_tensor_tensor(
            msk[:, ti * 8:(ti + 1) * 8], mk1[:, ti * 8:(ti + 1) * 8], -1e9,
            bia[:, ti * 8:(ti + 1) * 8], ALU.mult, ALU.add))
    r.raw("vector", lambda e: e.drain())
    for ti in range(NT):
        r.op("vector", lambda e, ti=ti: e.reduce_max(
            gm2[:, ti:ti + 1], msk[:, ti * 8:(ti + 1) * 8], axis=AX))
    r.raw("vector", lambda e: e.drain())
    for ti in range(NT):
        r.op("vector", lambda e, ti=ti: e.tensor_scalar(
            mk2[:, ti * 8:(ti + 1) * 8], msk[:, ti * 8:(ti + 1) * 8],
            gm2[:, ti:ti + 1], None, ALU.is_ge))
    r.raw("vector", lambda e: e.drain())
    for ti in range(NT):
        r.op("vector", lambda e, ti=ti: e.tensor_add(
            mk1[:, ti * 8:(ti + 1) * 8], mk1[:, ti * 8:(ti + 1) * 8],
            mk2[:, ti * 8:(ti + 1) * 8]))
    r.raw("vector", lambda e: e.drain())
    for ti in range(NT):
        incs = [(vd, 1)] if ti == NT - 1 else []
        r.op("vector", lambda e, ti=ti: e.tensor_mul(
            cwa[:, ti * 8:(ti + 1) * 8], sct[:, ti * 8:(ti + 1) * 8],
            mk1[:, ti * 8:(ti + 1) * 8]), incs)
    vd_cwt_box = [0]

    def emit_cwT():
        r.wait("tensor", vd, vd.n)
        for grp in range(4):
            b = banks[4 + grp % 2]
            b.acquire(r)
            for k in range(4):
                ti = grp * 4 + k
                incs = [(md, 1)] if k == 3 else []
                r.op("tensor", lambda e, ti=ti, k=k, b=b: e.transpose(
                    b.t[0:8, k * 128:(k + 1) * 128], cwa[:, ti * 8:(ti + 1) * 8],
                    identf[:, :]), incs)
            r.wait("vector", md, md.n)
            r.op("vector", lambda e, grp=grp, b=b: e.tensor_copy(
                cw_T[0:8, grp * 512:(grp + 1) * 512], b.t[0:8, 0:512]), [(vd, 1)])
            b.release(vd)
        vd_cwt_box[0] = vd.n

    bank_sets = [(0, 1, 2, 3), (4, 5, 0, 1), (2, 3, 4, 5)]
    gidx = [0]
    last_ap_vd = [0, 0, 0, 0]
    pre_ap_vd = [0, 0, 0, 0]
    pend = []
    sdml_hist = []

    def emit_hgroup(e_idx, j, qc, shared):
        i = gidx[0]
        gidx[0] += 1
        par = i % 2
        st = bank_sets[i % 3]
        bt = [banks[st[0]], banks[st[1]], banks[st[2]], banks[st[3]]]
        wsl, w13_ats, w13_sl = cur_w13[0]
        b1col = (sb1c if shared else b1c)[:, (0 if shared else e_idx) * 16 + j: (0 if shared else e_idx) * 16 + j + 1]
        b3col = (sb3c if shared else b3c)[:, (0 if shared else e_idx) * 16 + j: (0 if shared else e_idx) * 16 + j + 1]
        b1colB = (sb1c if shared else b1c)[:, (0 if shared else e_idx) * 16 + 8 + j: (0 if shared else e_idx) * 16 + 8 + j + 1]
        b3colB = (sb3c if shared else b3c)[:, (0 if shared else e_idx) * 16 + 8 + j: (0 if shared else e_idx) * 16 + 8 + j + 1]
        for src in range(4):
            b = bt[src]
            r.wait("tensor", w13s[w13_sl * 4 + src], w13_ats[src])
            b.acquire(r)
            for dk in range(ND):
                incs = [(md, 1)] if src == 3 and dk == ND - 1 else []
                r.op("tensor", lambda e, src=src, dk=dk, qc=qc, b=b, wsl=wsl: e.matmul(
                    b.t[:, :], wsl[:, (src * 8 + dk) * 128:(src * 8 + dk + 1) * 128],
                    xnf[:, dk * S + qc * 512: dk * S + qc * 512 + 512],
                    start=(dk == 0), stop=(dk == ND - 1)), incs)
        md_h = md.n
        if qc == 3:
            w13b.release(md)
        r.wait("vector", md, md_h)
        if len(sdml_hist) >= 2:
            r.wait("vector", sd, sdml_hist[-2])
        r.op("vector", lambda e, par=par, bt=bt, b3col=b3col: e.tensor_scalar(
            pa[par][:, :], bt[1].t[:, :], b3col, None, ALU.add), [(vd, 1)])
        bt[1].release(vd)
        r.op("vector", lambda e, par=par, bt=bt, b3colB=b3colB: e.tensor_scalar(
            pbx[par][:, :], bt[3].t[:, :], b3colB, None, ALU.add), [(vd, 1)])
        bt[3].release(vd)
        r.raw("vector", lambda e: e.drain())
        r.op("vector", lambda e, par=par, bt=bt, b1col=b1col: e.scalar_tensor_tensor(
            pa[par][:, :], bt[0].t[:, :], b1col, pa[par][:, :], ALU.add, ALU.mult),
            [(vd, 1)])
        bt[0].release(vd)
        r.op("vector", lambda e, par=par, bt=bt, b1colB=b1colB: e.scalar_tensor_tensor(
            pbx[par][:, :], bt[2].t[:, :], b1colB, pbx[par][:, :], ALU.add, ALU.mult),
            [(vd, 1)])
        bt[2].release(vd)
        r.raw("vector", lambda e: e.drain())
        if pend:
            pend.pop(0)(0)
        r.op("vector", lambda e, par=par: e.tensor_scalar(
            pa[par][:, :], pa[par][:, :], 7.0, None, ALU.min), [(vd, 1)])
        vd_g = vd.n
        r.raw("vector", lambda e: e.drain())
        if pend:
            pend.pop(0)(0)
        r.op("vector", lambda e, par=par: e.tensor_scalar(
            pbx[par][:, :], pbx[par][:, :], 7.0, -7.0, ALU.min, ALU.max), [(vd, 1)])
        vd_u = vd.n
        r.wait("scalar", vd, vd_g)
        r.op("scalar", lambda e, par=par: e.activation(
            mm_[par][:, :], pa[par][:, :], AF.Silu, scale=1.702))
        r.wait("scalar", vd, vd_u)
        r.op("scalar", lambda e, par=par: e.activation(
            lp_[par][:, :], pbx[par][:, :], AF.Identity, scale=1.0 / 1.702,
            bias=cInv[:, 0:1]), [(sd, 1)])
        sd_ml = sd.n
        sdml_hist.append(sd_ml)
        dst = a_sb[:, j * 2048 + qc * 512: j * 2048 + (qc + 1) * 512]

        def fin_aa(_=0, par=par, dst=dst, sd_ml=sd_ml, shared=shared, qc=qc, j=j):
            r.wait("vector", sd, sd_ml)
            if shared:
                r.op("vector", lambda e: e.tensor_mul(dst, mm_[par][:, :], lp_[par][:, :]),
                     [(vd, 1)])
                if j == 7:
                    last_ap_vd[qc] = vd.n
                elif j == 6:
                    pre_ap_vd[qc] = vd.n
            else:
                r.op("vector", lambda e: e.tensor_mul(aa_s[:, :], mm_[par][:, :], lp_[par][:, :]))

        def fin_ap(_=0, dst=dst, shared=shared, qc=qc, j=j):
            if not shared:
                r.op("vector", lambda e: e.tensor_mul(
                    dst, aa_s[:, :], cwb[:, qc * 512:(qc + 1) * 512]), [(vd, 1)])
                if j == 7:
                    last_ap_vd[qc] = vd.n
                elif j == 6:
                    pre_ap_vd[qc] = vd.n

        pend.append(fin_aa)
        pend.append(fin_ap)

    cur_w13 = [None]
    w13cnt = [0]

    def emit_expert(e_idx):
        shared = e_idx == EXP
        if not shared:
            r.wait("tensor", vd, vd_cwt_box[0])
            for qc in range(4):
                b = banks[6 + qc % 2]
                b.acquire(r)
                incs = [(md, 1)]
                r.op("tensor", lambda e, e_idx=e_idx, qc=qc, b=b: e.matmul(
                    b.t[:, :], selm[0:8, e_idx * 128:(e_idx + 1) * 128],
                    cw_T[0:8, qc * 512:(qc + 1) * 512], start=True, stop=True), incs)
                r.wait("vector", md, md.n)
                r.op("vector", lambda e, qc=qc, b=b: e.tensor_copy(
                    cwb[:, qc * 512:(qc + 1) * 512], b.t[:, :]), [(vd, 1)])
                b.release(vd)
            r.raw("vector", lambda e: e.drain())
        slabs = []

        def issue_w13(j):
            wsl = w13b.acquire(r, "gpsimd")
            i = w13cnt[0]
            w13cnt[0] += 1
            sl = i % 3
            src_ap = d["sw13_s"][j] if shared else d["w13_s"][e_idx, j]
            ats = []
            for part in range(4):
                ats.append(dma(
                    "gpsimd", wsl[:, part * 1024:(part + 1) * 1024],
                    src_ap[:, part * 1024:(part + 1) * 1024], w13s[sl * 4 + part]))
            slabs.append((wsl, ats, sl))

        if shared:
            slabs.append(staged_j0)
            w13cnt[0] += 1
        else:
            issue_w13(0)
        issue_w13(1)
        issue_w13(2)
        wv = w2b.acquire(r, "gpsimd")
        gd_w2 = dma("gpsimd", wv[:, :], d["sw2_s"] if shared else d["w2_s"][e_idx], w2s)
        for j in range(8):
            if j >= 3:
                issue_w13(j)
            cur_w13[0] = slabs[j]
            for qc in range(4):
                emit_hgroup(e_idx, j, qc, shared)
        while pend:
            pend.pop(0)(0)
            r.raw("vector", lambda e: e.drain())
        r.wait("tensor", w2s, gd_w2)

        def w2_mms(qc, dm, b, fks):
            first = fks[0] == 0
            if e_idx == 0 and first:
                r.op("tensor", lambda e, dm=dm, qc=qc, b=b: e.matmul(
                    b.t[:, :], b2c[0:8, dm * 128:(dm + 1) * 128],
                    cw_T[0:8, qc * 512:(qc + 1) * 512], start=True, stop=False))
                first = False
            for fk in fks:
                incs = [(md, 1)] if fk == 7 else []
                r.op("tensor", lambda e, dm=dm, qc=qc, fk=fk, b=b, wv=wv,
                     st=(first and fk == fks[0]): e.matmul(
                    b.t[:, :], wv[:, fk * 1024 + dm * 128: fk * 1024 + (dm + 1) * 128],
                    a_sb[:, fk * 2048 + qc * 512: fk * 2048 + (qc + 1) * 512],
                    start=st, stop=(fk == 7)), incs)

        def w2_evac(qc, dm, b):
            r.wait("vector", md, md.n)
            r.op("vector", lambda e, dm=dm, qc=qc, b=b: e.tensor_add(
                acc[:, dm * S + qc * 512: dm * S + (qc + 1) * 512],
                acc[:, dm * S + qc * 512: dm * S + (qc + 1) * 512], b.t[:, :]),
                [(vd, 1)])
            b.release(vd)

        s1 = bank_sets[(gidx[0] - 2) % 3]
        s2 = bank_sets[(gidx[0] - 1) % 3]
        extra = [bi for bi in s1 if bi not in s2][:2]
        pre = [banks[6], banks[7], banks[extra[0]], banks[extra[1]]]
        r.wait("tensor", vd, pre_ap_vd[0])
        for dm in range(4):
            pre[dm].acquire(r)
            w2_mms(0, dm, pre[dm], list(range(7)))
        r.wait("tensor", vd, last_ap_vd[0])
        for dm in range(4):
            w2_mms(0, dm, pre[dm], [7])
            w2_evac(0, dm, pre[dm])
        for qc in range(4):
            r.wait("tensor", vd, last_ap_vd[qc])
            for dm in range(4 if qc == 0 else 0, ND):
                b = banks[6 + (qc * 8 + dm) % 2]
                b.acquire(r)
                w2_mms(qc, dm, b, list(range(8)))
                if dm == ND - 1 and qc == 3:
                    w2b.release(md)
                w2_evac(qc, dm, b)

    emit_expert(EXP)
    r.wait("tensor", gdf, gd_fconsts)
    emit_cwT()
    for e_idx in range(EXP):
        emit_expert(e_idx)
    vd_ffn_done = vd.n
    r.flush(nc)
    es3.close()

    es4 = contextlib.ExitStack()
    otb = Buf(nc, es4, "otb", 2, [128, 1024], F32)
    r.wait("tensor", vd, vd_ffn_done)
    for ti in range(NT):
        ot = otb.acquire(r, "vector")
        bA = banks[(2 * ti) % 8]
        bB = banks[(2 * ti + 1) % 8]
        bA.acquire(r)
        bB.acquire(r)
        for dk in range(ND):
            b = bA if dk < 4 else bB
            incs = [(md, 1)] if dk % 4 == 3 else []
            r.op("tensor", lambda e, dk=dk, ti=ti, b=b: e.transpose(
                b.t[:, (dk % 4) * 128:(dk % 4 + 1) * 128],
                acc[:, dk * S + ti * 128: dk * S + (ti + 1) * 128],
                identf[:, :]), incs)
        r.wait("vector", md, md.n)
        r.op("vector", lambda e, ot=ot, bA=bA: e.tensor_copy(
            ot[:, 0:512], bA.t[:, :]), [(vd, 1)])
        bA.release(vd)
        r.op("vector", lambda e, ot=ot, bB=bB: e.tensor_copy(
            ot[:, 512:1024], bB.t[:, :]), [(vd, 1)])
        bB.release(vd)
        r.wait("sync", vd, vd.n)
        dma("sync", out_hbm[ti * 128:(ti + 1) * 128, :], ot[:, :], ods[ti % 2])
        otb.release(ods[ti % 2])
    r.wait("sync", ods[0], ods[0].n)
    r.wait("sync", ods[1], ods[1].n)
    r.flush(nc)
    es4.close()
    esP.close()
    es0.close()
    return nc


def kernel(**inputs):
    x_all = np.asarray(inputs["L0_attn_x"], dtype=np.float32)
    lay = host_layouts(inputs)
    nc = build_nc()
    in_maps = []
    for c in range(8):
        m = {"x": np.ascontiguousarray(x_all[c])}
        m.update(lay)
        in_maps.append(m)
    res = run_bass_kernel_spmd(nc, in_maps, core_ids=list(range(8)))
    return np.stack([np.asarray(r["out"], dtype=np.float32) for r in res.results], axis=0)


if __name__ == "__main__":
    import reference as _ref
    inp = _ref.setup_inputs()
    out = kernel(**{k: np.asarray(v) for k, v in inp.items()})
    exp = np.asarray(_ref.reference(**inp))
    rel = np.linalg.norm(out - exp) / np.linalg.norm(exp)
    print("rel:", rel)


# version 22
# speedup vs baseline: 1.0159x; 1.0159x over previous
import contextlib
import numpy as np
import ml_dtypes
import concourse.bass as bass
import concourse.mybir as mybir
from concourse.bass_utils import run_bass_kernel_spmd

F32 = mybir.dt.float32
BF16 = mybir.dt.bfloat16
AF = mybir.ActivationFunctionType
ALU = mybir.AluOpType
AX = mybir.AxisListType.X
BFD = ml_dtypes.bfloat16

S = 2048
DMODEL = 1024
NT = 16
ND = 8
NP = 8
EXP = 8
CSHIFT = 12.0

ENGS = ("sync", "gpsimd", "tensor", "vector", "scalar")


class Sem:
    def __init__(self, nc, es, name, owner=None):
        self.h = es.enter_context(nc.semaphore(name))
        self.n = 0
        self.owner = owner


class Rec:
    def __init__(self):
        self.ops = {e: [] for e in ENGS}
        self.last_wait = {}

    def op(self, eng, fn, incs=()):
        incs = tuple(incs)
        main, extra = incs[:2], incs[2:]
        if extra:
            assert eng in ("vector", "scalar", "gpsimd"), "no engine_nop overflow on " + eng

        def f(e, fn=fn, main=main, extra=extra):
            i = fn(e)
            for s, k in main:
                i.then_inc(s.h, k)
            while extra:
                chunk, extra = extra[:2], extra[2:]
                j = e.engine_nop()
                for s, k in chunk:
                    j.then_inc(s.h, k)
        self.ops[eng].append(f)
        for s, k in incs:
            s.n += k

    def wait(self, eng, sem, val=None):
        if val is None:
            val = sem.n
        if val <= 0 or sem.owner == eng:
            return
        key = (eng, id(sem))
        if self.last_wait.get(key, -1) >= val:
            return
        self.last_wait[key] = val
        self.ops[eng].append(lambda e, s=sem, v=val: e.wait_ge(s.h, v))

    def raw(self, eng, fn):
        self.ops[eng].append(fn)

    def flush(self, nc):
        with nc.Block() as block:
            for name in ENGS:
                ops = self.ops[name]
                if ops:
                    getattr(block, name)(lambda e, ops=list(ops): [f(e) for f in ops])
        self.ops = {e: [] for e in ENGS}


class Rel:

    def __init__(self):
        self.evts = []
        self.acq = 0

    def acquire(self, r, eng):
        assert len(self.evts) >= self.acq, "acquire before prior release recorded"
        if self.acq > 0:
            sem, val = self.evts[self.acq - 1]
            r.wait(eng, sem, val)
        self.acq += 1

    def release(self, sem, val=None):
        self.evts.append((sem, val if val is not None else sem.n))


class Bank(Rel):
    def __init__(self, nc, es, idx):
        super().__init__()
        self.t = es.enter_context(nc.psum_tensor(f"pbank{idx}", [128, 512], F32))

    def acquire(self, r, eng="tensor"):
        super().acquire(r, eng)


class SubBank(Rel):
    def __init__(self, region, lo):
        super().__init__()
        self.t = region[:, lo:lo + 512]

    def acquire(self, r, eng="tensor"):
        super().acquire(r, eng)


class Buf(Rel):
    def __init__(self, nc, es, name, n, shape, dtype):
        super().__init__()
        self.ts = [es.enter_context(nc.sbuf_tensor(f"{name}{i}", shape, dtype))
                   for i in range(n)]
        self.nb = n

    def acquire(self, r, eng):
        u = self.acq
        if u >= self.nb:
            sem, val = self.evts[u - self.nb]
            r.wait(eng, sem, val)
        self.acq += 1
        return self.ts[u % self.nb]

    def reserve(self):
        u = self.acq
        w = self.evts[u - self.nb] if u >= self.nb else None
        self.acq += 1
        return self.ts[u % self.nb], w


def host_layouts(inp):
    f32 = np.float32
    a = {k: np.asarray(v, dtype=f32) for k, v in inp.items()}
    out = {}

    wq = a["L0_attn_w_qkv"].reshape(1024, 3072)
    out["wqkv_s"] = np.ascontiguousarray(
        wq.reshape(8, 128, 3, 8, 128).transpose(3, 1, 2, 0, 4)).reshape(8, 128, 3072)
    bq = a["L0_attn_b_qkv"].reshape(3072)
    out["bqkT"] = np.ascontiguousarray(bq.reshape(24, 128).T)
    out["brow_v"] = np.ascontiguousarray(bq[2048:].reshape(1, 1024))
    out["wout_s"] = np.ascontiguousarray(a["L0_attn_w_out"].reshape(1024, 1024))
    out["bout_s"] = np.ascontiguousarray(a["L0_attn_b_out"].reshape(8, 128).T)

    out["gate_ws"] = np.ascontiguousarray(
        a["L0_ffn_gate_w"].T.reshape(8, 128, 8).transpose(1, 0, 2).reshape(128, 64))
    out["gate_b"] = np.ascontiguousarray(np.broadcast_to(a["L0_ffn_gate_b"], (128, 8)))

    perm = np.concatenate([np.arange(0, 2048, 2), np.arange(1, 2048, 2)])

    def prep13(w):
        wp = w[..., perm, :]
        wt = np.swapaxes(wp, -1, -2)
        lead = wt.shape[:-2]
        t = wt.reshape(lead + (8, 128, 16, 128))
        t = np.moveaxis(t, -2, -4)
        t = np.swapaxes(t, -3, -2)
        return t

    def w13_fused(w1, w3):
        t1, t3 = prep13(w1), prep13(w3)
        A1, B1 = t1[..., 0:8, :, :, :], t1[..., 8:16, :, :, :]
        A3, B3 = t3[..., 0:8, :, :, :], t3[..., 8:16, :, :, :]
        st = np.stack([A1, A3, B1, B3], axis=-3)
        lead = st.shape[:-4]
        return np.ascontiguousarray(st.reshape(lead + (128, 4096)))

    out["w13_s"] = w13_fused(a["L0_ffn_w1"], a["L0_ffn_w3"])
    out["sw13_s"] = w13_fused(a["L0_ffn_sw1"][None], a["L0_ffn_sw3"][None])[0]

    def bperm(b):
        bp = b[..., perm]
        lead = bp.shape[:-1]
        t = bp.reshape(lead + (16, 128))
        t = np.moveaxis(t, -1, 0)
        return np.ascontiguousarray(t.reshape((128, -1)))
    out["b1_s"] = bperm(a["L0_ffn_b1"])
    out["b3_s"] = bperm(a["L0_ffn_b3"])
    out["sb1_s"] = bperm(a["L0_ffn_sb1"])
    out["sb3_s"] = bperm(a["L0_ffn_sb3"])

    def w2_slab(w2):
        wt = np.swapaxes(w2, -1, -2)
        lead = wt.shape[:-2]
        t = wt.reshape(lead + (8, 128, 1024)).swapaxes(-3, -2)
        return np.ascontiguousarray(t.reshape(lead + (128, 8192)))
    out["w2_s"] = w2_slab(a["L0_ffn_w2"])
    out["sw2_s"] = w2_slab(a["L0_ffn_sw2"][None])[0]
    out["b2_s"] = np.ascontiguousarray(a["L0_ffn_b2"])
    out["sb2_s"] = np.ascontiguousarray(a["L0_ffn_sb2"].reshape(8, 128).T)
    return out


def build_nc():
    nc = bass.Bass("TRN2", target_bir_lowering=False)
    r = Rec()

    d = {}
    for k, shape in {
        "x": [S, DMODEL], "bqkT": [128, 24], "bout_s": [128, 8], "sb2_s": [128, 8],
        "gate_b": [128, 8], "b1_s": [128, 128], "b3_s": [128, 128],
        "sb1_s": [128, 16], "sb3_s": [128, 16],
        "wqkv_s": [8, 128, 3072], "brow_v": [1, 1024], "wout_s": [1024, 1024],
        "gate_ws": [128, 64], "w13_s": [EXP, 8, 128, 4096], "w2_s": [EXP, 128, 8192],
        "sw13_s": [8, 128, 4096], "sw2_s": [128, 8192], "b2_s": [8, 1024],
    }.items():
        d[k] = nc.dram_tensor(k, shape, F32, kind="ExternalInput").ap()
    out_hbm = nc.dram_tensor("out", [S, DMODEL], F32, kind="ExternalOutput").ap()

    tri_np = np.triu(np.ones((128, 128), dtype=BFD))
    amask_np = ((1.0 - np.triu(np.ones((128, 128)))) * -3e9).astype(BFD)
    identb_np = np.eye(128, dtype=BFD)
    identf_np = np.eye(128, dtype=np.float32)
    sel_np = np.zeros((8, 8 * 128), dtype=BFD)
    for e in range(8):
        sel_np[e, e * 128:(e + 1) * 128] = 1
    sel2_np = np.zeros((33, 128), dtype=BFD)
    sel2_np[0, 0:64] = 1
    sel2_np[32, 64:128] = 1
    d_sel2 = nc.inline_tensor(sel2_np, name="sel2_c").ap()
    d_tri = nc.inline_tensor(tri_np, name="tri_c").ap()
    d_amask = nc.inline_tensor(amask_np, name="amask_c").ap()
    d_idb = nc.inline_tensor(identb_np, name="identb_c").ap()
    d_idf = nc.inline_tensor(identf_np, name="identf_c").ap()
    d_sel = nc.inline_tensor(sel_np, name="sel_c").ap()

    es0 = contextlib.ExitStack()
    sb0 = lambda name, shape, dtype: es0.enter_context(nc.sbuf_tensor(name, shape, dtype))

    xnf = sb0("xnf", [128, ND * S], BF16)
    acc = sb0("acc", [128, ND * S], F32)
    bqkT = sb0("bqkTs", [128, 24], F32)
    bout = sb0("bouts", [128, 8], F32)
    sb2c = sb0("sb2cs", [128, 8], F32)
    binit = sb0("binit", [128, 8], F32)
    negC = sb0("negC", [128, 1], F32)
    cInv = sb0("cInv", [128, 1], F32)
    identf = sb0("identf", [128, 128], F32)
    identb = sb0("identb", [128, 128], BF16)
    tri = sb0("tri", [128, 128], BF16)
    amask = sb0("amask", [128, 128], BF16)
    ones_bf = sb0("ones_bf", [1, 128], BF16)
    brow_v = sb0("brow_vs", [1, 1024], BF16)

    sx = Sem(nc, es0, "sx", "sync")
    gdc = Sem(nc, es0, "gdc", "gpsimd")
    gdw = Sem(nc, es0, "gdw", "gpsimd")
    gdf = Sem(nc, es0, "gdf", "gpsimd")
    gdg = Sem(nc, es0, "gdg", "gpsimd")
    wqs = [Sem(nc, es0, f"wqs{i}", "gpsimd") for i in range(6)]
    w13s = [Sem(nc, es0, f"w13s{i}", "gpsimd") for i in range(12)]
    w2s = Sem(nc, es0, "w2s", "gpsimd")
    ods = [Sem(nc, es0, f"ods{i}", "sync") for i in range(2)]
    sxs_ = [Sem(nc, es0, f"sxc{i}", "sync") for i in range(4)]
    md = Sem(nc, es0, "md", "tensor")
    vd = Sem(nc, es0, "vd", "vector")
    sd = Sem(nc, es0, "sd", "scalar")

    def dma(q, dst, src, sem):
        r.op(q, lambda e, dst=dst, src=src: e.dma_start(dst, src), [(sem, 16)])
        return sem.n

    for nm, t in [("bqkT", bqkT), ("bout_s", bout), ("sb2_s", sb2c)]:
        dma("gpsimd", t[:, :], d[nm], gdc)
    dma("gpsimd", brow_v[0:1, :], d["brow_v"], gdc)
    dma("gpsimd", tri[:, :], d_tri, gdc)
    dma("gpsimd", amask[:, :], d_amask, gdc)
    dma("gpsimd", identf[:, :], d_idf, gdc)
    dma("gpsimd", identb[:, :], d_idb, gdc)
    gd_consts = gdc.n

    r.op("vector", lambda e: e.memset(ones_bf[0:1, :], 1.0))
    r.op("vector", lambda e: e.memset(negC[:, :], -float(CSHIFT)))
    r.op("vector", lambda e: e.memset(cInv[:, :], 1.0 / 1.702), [(vd, 1)])
    vd_consts = vd.n
    r.wait("vector", gdc, gd_consts)
    r.op("vector", lambda e: e.tensor_add(binit[:, :], bout[:, :], sb2c[:, :]), [(vd, 1)])
    vd_binit = vd.n

    es1 = contextlib.ExitStack()
    sb1_ = lambda name, shape, dtype: es1.enter_context(nc.sbuf_tensor(name, shape, dtype))
    x_tm = sb1_("x_tm", [128, NT * 1024], F32)
    xn_tm = Buf(nc, es1, "xn_tm", 2, [128, 1024], BF16)
    bst = sb1_("bst", [128, NT * 12], F32)
    mv = sb1_("mv", [128, NT * 2], F32)
    stv = sb1_("stv", [128, 16], F32)
    str_ = sb1_("str", [128, 16], F32)
    stn = sb1_("stn", [128, 16], F32)
    pb_t = [es1.enter_context(nc.psum_tensor(f"pbf{i}", [128, 1024], BF16)) for i in range(2)]
    pf_t = [es1.enter_context(nc.psum_tensor(f"pff{i}", [128, 512], F32)) for i in range(4)]
    pbR = [Rel(), Rel()]
    pfR = [Rel(), Rel(), Rel(), Rel()]

    for ch in range(4):
        dma("sync",
            x_tm[:, ch * 4096:(ch + 1) * 4096].rearrange("p (t c) -> p t c", t=4),
            d["x"][ch * 512:(ch + 1) * 512, :].rearrange("(t p) c -> p t c", p=128),
            sxs_[ch])
    for ti in range(NT):
        r.wait("vector", sxs_[ti // 4], 16)
        r.op("vector", lambda e, ti=ti: e.bn_stats(
            bst[:, ti * 12: ti * 12 + 6], x_tm[:, ti * 1024: ti * 1024 + 512]))
        r.op("vector", lambda e, ti=ti: e.bn_stats(
            bst[:, ti * 12 + 6: ti * 12 + 12], x_tm[:, ti * 1024 + 512: (ti + 1) * 1024]))
    r.raw("vector", lambda e: e.drain())
    for ti in range(NT):
        r.op("vector", lambda e, ti=ti: e.bn_aggr(
            mv[:, ti * 2: ti * 2 + 2], bst[:, ti * 12: (ti + 1) * 12]))
    r.raw("vector", lambda e: e.drain())
    mv3 = mv[:, :].rearrange("p (t two) -> p t two", two=2)
    r.op("vector", lambda e: e.tensor_scalar(
        stv[:, :].rearrange("p (t one) -> p t one", one=1), mv3[:, :, 1:2],
        1e-5, None, ALU.add), [(vd, 1)])
    r.wait("scalar", vd, vd.n)
    r.op("scalar", lambda e: e.activation(stv[:, :], stv[:, :], AF.Sqrt), [(sd, 1)])
    r.wait("vector", sd, sd.n)
    r.op("vector", lambda e: e.reciprocal(str_[:, :], stv[:, :]))
    r.raw("vector", lambda e: e.drain())
    r.op("vector", lambda e: e.scalar_tensor_tensor(
        stn[:, :].rearrange("p (t one) -> p t one", one=1), mv3[:, :, 0:1], -1.0,
        str_[:, :].rearrange("p (t one) -> p t one", one=1),
        ALU.mult, ALU.mult), [(vd, 1)])
    r.raw("vector", lambda e: e.drain())
    vd_stats = vd.n

    vd_xn_at = [0] * NT
    for ti in range(NT):
        xb = xn_tm.acquire(r, "vector")
        r.wait("vector", vd, vd_stats)
        r.op("vector", lambda e, ti=ti, xb=xb: e.tensor_scalar(
            xb[:, :], x_tm[:, ti * 1024:(ti + 1) * 1024],
            str_[:, ti:ti + 1], stn[:, ti:ti + 1], ALU.mult, ALU.add), [(vd, 1)])
        r.wait("tensor", vd, vd.n)
        pb = pb_t[ti % 2]
        pbR[ti % 2].acquire(r, "tensor")
        r.wait("tensor", gdc, gd_consts)
        for dk in range(ND):
            incs = [(md, 1)] if dk == ND - 1 else []
            r.op("tensor", lambda e, dk=dk, xb=xb, pb=pb: e.transpose(
                pb[:, dk * 128:(dk + 1) * 128], xb[:, dk * 128:(dk + 1) * 128],
                identb[:, :]), incs)
        md_xn = md.n
        xn_tm.release(md)
        pfb = (ti % 2) * 2
        pfR[pfb].acquire(r, "tensor")
        pfR[pfb + 1].acquire(r, "tensor")
        for dk in range(ND):
            incs = [(md, 1)] if dk % 4 == 3 else []
            r.op("tensor", lambda e, dk=dk, ti=ti, pfb=pfb: e.transpose(
                pf_t[pfb + dk // 4][:, (dk % 4) * 128:(dk % 4 + 1) * 128],
                x_tm[:, ti * 1024 + dk * 128: ti * 1024 + (dk + 1) * 128],
                identf[:, :]), incs)
        md_x = md.n
        r.wait("vector", md, md_xn)
        r.op("vector", lambda e, ti=ti, pb=pb: e.tensor_copy(
            xnf[:, :].rearrange("p (dk s) -> p dk s", dk=ND)[:, :, ti * 128:(ti + 1) * 128],
            pb[:, :].rearrange("p (dk c) -> p dk c", dk=ND)), [(vd, 1)])
        vd_xn_at[ti] = vd.n
        pbR[ti % 2].release(vd)
        r.wait("scalar", md, md_x)
        r.wait("scalar", vd, vd_binit)
        for dk in range(ND):
            incs = [(sd, 1)] if dk == 3 or dk == 7 else []
            r.op("scalar", lambda e, dk=dk, ti=ti, pfb=pfb: e.activation(
                acc[:, dk * S + ti * 128: dk * S + (ti + 1) * 128],
                pf_t[pfb + dk // 4][:, (dk % 4) * 128:(dk % 4 + 1) * 128], AF.Identity,
                bias=binit[:, dk:dk + 1]), incs)
            if dk == 3:
                pfR[pfb].release(sd)
            if dk == 7:
                pfR[pfb + 1].release(sd)
    vd_xn_done = vd.n
    sd_acc_init = sd.n
    r.flush(nc)
    es1.close()

    esP = contextlib.ExitStack()
    banks = [Bank(nc, esP, i) for i in range(4)]
    reg45 = esP.enter_context(nc.psum_tensor("reg45", [128, 1024], F32))
    reg67 = esP.enter_context(nc.psum_tensor("reg67", [128, 1024], F32))
    banks = banks + [SubBank(reg45, 0), SubBank(reg45, 512),
                     SubBank(reg67, 0), SubBank(reg67, 512)]
    regions = [(reg45, banks[4], banks[5]), (reg67, banks[6], banks[7])]
    rot_i = [0]

    def rot_next():
        reg = regions[rot_i[0] % 2]
        rot_i[0] += 1
        reg[1].acquire(r)
        reg[2].acquire(r)
        return reg

    def rot_release(reg, sem):
        reg[1].release(sem)
        reg[2].release(sem)

    es2 = contextlib.ExitStack()
    wqkvb = Buf(nc, es2, "wqkvb", 2, [128, 3072], BF16)
    qkb = Buf(nc, es2, "qkb", 2, [128, 4096], BF16)
    vhb = Buf(nc, es2, "vhb", 2, [128, 16 * 160], BF16)
    ptb = Buf(nc, es2, "ptb", 4, [128, 1024], BF16)
    lvf = Buf(nc, es2, "lvf", 2, [128, 512], F32)
    lvb = Buf(nc, es2, "lvb", 2, [64, 512], BF16)
    lvc = Buf(nc, es2, "lvc", 2, [128, 512], F32)
    ansT = es2.enter_context(nc.sbuf_tensor("ansT", [128, 8 * 2048], BF16))
    woutb = es2.enter_context(nc.sbuf_tensor("woutb", [128, 8192], BF16))
    sel2 = es2.enter_context(nc.sbuf_tensor("sel2", [33, 128], BF16))
    gd_sel2 = dma("gpsimd", sel2[0:33, :], d_sel2, gdc)
    for i in range(2):
        r.op("vector", lambda e, i=i: e.memset(lvf.ts[i][0:64, :], 1.0))
    r.raw("vector", lambda e: e.drain())

    for i in range(2):
        r.op("vector", lambda e, i=i: e.memset(
            vhb.ts[i][:, :].rearrange("p (kt z) -> p kt z", z=160)[:, :, 64:65], 1.0))
        r.op("vector", lambda e, i=i: e.memset(
            vhb.ts[i][:, :].rearrange("p (kt z) -> p kt z", z=160)[:, :, 144:145], 1.0),
            [(vd, 1)])
    vd_vones = vd.n
    gd_wout_box = [0]

    pair_sd_done = {}

    def make_qkv_jobs(p):
        wt = wqkvb.acquire(r, "gpsimd")
        qb = (p % 2) * 3
        w_ats = [dma("gpsimd", wt[:, part * 1024:(part + 1) * 1024],
                     d["wqkv_s"][p][:, part * 1024:(part + 1) * 1024],
                     wqs[qb + part]) for part in range(3)]
        qt, qk_w = qkb.reserve()
        vt, vh_w = vhb.reserve()
        jobs = []

        def qk_job(tq, qc, first):
            def f():
                if qc == 0:
                    r.wait("tensor", wqs[qb + tq], w_ats[tq])
                if first:
                    if qk_w:
                        r.wait("scalar", *qk_w)
                r.wait("tensor", vd, vd_xn_at[qc * 4 + 3])
                rr = rot_next()
                for dk in range(ND):
                    incs = [(md, 1)] if dk == ND - 1 else []
                    r.op("tensor", lambda e, tq=tq, dk=dk, qc=qc, rr=rr, wt=wt: e.matmul(
                        rr[0][:, 0:512], wt[:, (tq * 8 + dk) * 128:(tq * 8 + dk + 1) * 128],
                        xnf[:, dk * S + qc * 512: dk * S + qc * 512 + 512],
                        start=(dk == 0), stop=(dk == ND - 1)), incs)
                r.wait("scalar", md, md.n)
                r.op("scalar", lambda e, tq=tq, qc=qc, rr=rr, qt=qt, p=p: e.activation(
                    qt[:, tq * 2048 + qc * 512: tq * 2048 + (qc + 1) * 512], rr[0][:, 0:512],
                    AF.Identity, bias=bqkT[:, tq * 8 + p: tq * 8 + p + 1]), [(sd, 1)])
                rot_release(rr, sd)
            return f

        def v_job(kt, first):
            def f():
                if first:
                    r.wait("tensor", wqs[qb + 2], w_ats[2])
                    r.wait("tensor", vd, vd_vones)
                    if vh_w:
                        r.wait("scalar", *vh_w)
                r.wait("tensor", vd, vd_xn_at[kt])
                rr = rot_next()
                for dk in range(ND):
                    r.op("tensor", lambda e, kt=kt, dk=dk, rr=rr, wt=wt: e.matmul(
                        rr[0][:, 0:128], xnf[:, dk * S + kt * 128: dk * S + (kt + 1) * 128],
                        wt[:, (16 + dk) * 128:(16 + dk + 1) * 128],
                        start=(dk == 0), stop=False))
                r.op("tensor", lambda e, rr=rr, p=p: e.matmul(
                    rr[0][:, 0:128], ones_bf[0:1, 0:128], brow_v[0:1, p * 128:(p + 1) * 128],
                    start=False, stop=True), [(md, 1)])
                r.wait("scalar", md, md.n)
                r.op("scalar", lambda e, kt=kt, rr=rr, vt=vt: e.copy(
                    vt[:, kt * 160:kt * 160 + 160].rearrange("p (h z) -> p h z", z=80)[:, :, 0:64],
                    rr[0][:, 0:128].rearrange("p (h z) -> p h z", z=64)), [(sd, 1)])
                rot_release(rr, sd)
                if kt == NT - 1:
                    wqkvb.release(sd)
                    pair_sd_done[p] = sd.n
            return f

        for tq in (0, 1):
            for qc in range(4):
                jobs.append(qk_job(tq, qc, tq == 0 and qc == 0))
        for kt in range(NT):
            jobs.append(v_job(kt, kt == 0))
        return jobs, qt, vt

    def emit_attn(p, qt, vt, pump):
        r.wait("tensor", sd, pair_sd_done[p])
        for half in (0, 1):
            q0 = half * 1024
            nki = 8 if half == 0 else 16
            exp_at = [[0] * nki for _ in range(2)]
            pt_of = [[None] * nki for _ in range(2)]

            def emit_scores(hl, ki, half=half, q0=q0):
                off = hl * 64
                pt = ptb.acquire(r, "scalar")
                pt_of[hl][ki] = pt
                rr = rot_next()
                reg = rr[0]
                qs0 = max(ki * 128, q0)
                have_diag = ki * 128 >= q0
                diag_hb = (ki * 128 - q0) // 512 if have_diag else -1
                for hb in (0, 1):
                    lo = max(qs0, q0 + 512 * hb)
                    hi = q0 + 512 * (hb + 1)
                    if lo >= hi:
                        continue
                    din = diag_hb == hb
                    lastb = hb == 1 or (hb == 0 and qs0 >= q0 + 512)
                    r.op("tensor", lambda e, ki=ki, lo=lo, n=hi - lo, off=off, q0=q0,
                         reg=reg, din=din: e.matmul(
                        reg[:, lo - q0: lo - q0 + n],
                        qt[off:off + 64, 2048 + ki * 128: 2048 + (ki + 1) * 128],
                        qt[off:off + 64, lo:lo + n], start=True, stop=not din),
                        [(md, 1)] if (not din and hb == 1) else [])
                    if din:
                        r.op("tensor", lambda e, ki=ki, q0=q0, reg=reg: e.matmul(
                            reg[:, ki * 128 - q0: ki * 128 - q0 + 128], identb[:, :],
                            amask[:, :], start=False, stop=True),
                            [(md, 1)] if hb == 1 else [])
                if (diag_hb == 0 and qs0 < q0 + 512) or False:
                    pass
                last_sc = hl == 1 and half == 1 and ki == NT - 1
                if last_sc:
                    qkb.release(md)
                r.wait("scalar", md, md.n)
                r.op("scalar", lambda e, qs0=qs0, q0=q0, pt=pt, reg=reg: e.activation(
                    pt[:, qs0 - q0: 1024], reg[:, qs0 - q0: 1024], AF.Exp, scale=0.125,
                    bias=negC[:, 0:1]), [(sd, 1)])
                rot_release(rr, sd)
                exp_at[hl][ki] = sd.n

            def emit_av(hl, ki, half=half, q0=q0):
                off = hl * 64
                slot = hl * 80
                if ki == 0:
                    while evac_pend:
                        evac_pend.pop(0)[1]()
                    for lqc in range(2):
                        banks[hl * 2 + lqc].acquire(r)
                pt = pt_of[hl][ki]
                r.wait("tensor", sd, exp_at[hl][ki])
                live = [lqc for lqc in range(2)
                        if ki * 128 < (half * 2 + lqc + 1) * 512]
                for lqc in live:
                    qc = half * 2 + lqc
                    cs = max(0, ki * 128 - qc * 512)
                    n = 512 - cs
                    last_of_qc = ki == min(NT - 1, 4 * qc + 3)
                    incs = [(md, 1)] if lqc == live[-1] else []
                    r.op("tensor", lambda e, hl=hl, lqc=lqc, qc=qc, ki=ki, cs=cs, n=n,
                         pt=pt, slot=slot, q0=q0, last=last_of_qc: e.matmul(
                        banks[hl * 2 + lqc].t[0:65, cs:cs + n],
                        vt[:, ki * 160 + slot: ki * 160 + slot + 65],
                        pt[:, qc * 512 + cs - q0: qc * 512 + cs - q0 + n],
                        start=(ki == 0), stop=last), incs)
                ptb.release(md)

            def emit_evac(lqc, half=half):
                qc = half * 2 + lqc
                bk0 = banks[lqc]
                bk1 = banks[2 + lqc]
                lf = lvf.acquire(r, "vector")
                lb = lvb.acquire(r, "vector")
                r.wait("vector", md, md.n)
                r.op("vector", lambda e, bk0=bk0, lf=lf: e.tensor_copy(
                    lf[0:1, :], bk0.t[64:65, 0:512]))
                r.op("vector", lambda e, bk1=bk1, lf=lf: e.tensor_copy(
                    lf[32:33, :], bk1.t[64:65, 0:512]))
                r.raw("vector", lambda e: e.drain())

                def _recip(e, lf=lf, lb=lb):
                    with nc.allow_low_precision(reason="bf16 store of recip"):
                        e.reciprocal(lb[0:64, :], lf[0:64, :])
                r.op("vector", _recip)
                r.raw("vector", lambda e: e.drain())
                r.op("vector", lambda e: e.engine_nop(), [(vd, 1)])
                lvf.release(vd)
                vd_lb = vd.n

                def rest(qc=qc, lb=lb, vd_lb=vd_lb, p=p, bk0=bk0, bk1=bk1):
                    rr = rot_next()
                    r.wait("tensor", gdc, gd_sel2)
                    r.wait("tensor", vd, vd_lb)
                    r.op("tensor", lambda e, lb=lb, rr=rr: e.matmul(
                        rr[0][:, 0:512], sel2[0:33, :], lb[0:33, 0:512],
                        start=True, stop=True), [(md, 1)])
                    lvb.release(md)
                    r.wait("vector", md, md.n)
                    lc = lvc.acquire(r, "vector")
                    r.op("vector", lambda e, rr=rr, lc=lc: e.tensor_copy(
                        lc[:, :], rr[0][:, 0:512]), [(vd, 1)])
                    rot_release(rr, vd)
                    r.raw("vector", lambda e: e.drain())
                    r.op("vector", lambda e, qc=qc, lc=lc, p=p, bk0=bk0: e.tensor_mul(
                        ansT[0:64, p * 2048 + qc * 512: p * 2048 + (qc + 1) * 512],
                        bk0.t[0:64, 0:512], lc[0:64, :]), [(vd, 1)])
                    bk0.release(vd)
                    r.op("vector", lambda e, qc=qc, lc=lc, p=p, bk1=bk1: e.tensor_mul(
                        ansT[64:128, p * 2048 + qc * 512: p * 2048 + (qc + 1) * 512],
                        bk1.t[0:64, 0:512], lc[64:128, :]), [(vd, 1)])
                    bk1.release(vd)
                    lvc.release(vd)
                evac_pend.append((step_box[0], rest))

            for hl in (0, 1):
                emit_scores(hl, 0)
            if half == 1:
                pump()
            for ki in range(nki):
                step_box[0] += 1
                if ki + 1 < nki:
                    emit_scores(0, ki + 1)
                emit_av(0, ki)
                if ki + 1 < nki:
                    emit_scores(1, ki + 1)
                emit_av(1, ki)
                if half == 1 and ki == NT - 1:
                    vhb.release(md)
                lqc_done = None
                if ki == 4 * (half * 2) + 3:
                    lqc_done = 0
                elif ki == min(NT - 1, 4 * (half * 2 + 1) + 3):
                    lqc_done = 1
                if lqc_done is not None:
                    emit_evac(lqc_done)
                pump()
                if (lqc_done is None and evac_pend
                        and step_box[0] - evac_pend[0][0] >= 2):
                    evac_pend.pop(0)[1]()

    evac_pend = []
    step_box = [0]
    jobs0, qt0, vt0 = make_qkv_jobs(0)
    gd_wout_box[0] = dma(
        "gpsimd", woutb[:, :].rearrange("p (hek dm) -> p hek dm", hek=8),
        d["wout_s"].rearrange("(hek p) dm -> p hek dm", p=128), gdw)
    for jb in jobs0:
        jb()
    prev = (qt0, vt0)
    pending = []

    def pump():
        if pending:
            pending.pop(0)()

    for p in range(NP):
        if p + 1 < NP:
            jobs, qtn, vtn = make_qkv_jobs(p + 1)
            pending.extend(jobs)
            nxt = (qtn, vtn)
        else:
            nxt = None
        emit_attn(p, prev[0], prev[1], pump)
        while pending:
            pending.pop(0)()
        prev = nxt

    while evac_pend:
        evac_pend.pop(0)[1]()
    r.wait("tensor", gdw, gd_wout_box[0])
    r.wait("tensor", vd, vd.n)
    for dm in range(ND):
        for qc in range(4):
            b = banks[(dm % 2) * 4 + qc]
            b.acquire(r)
            for hek in range(8):
                incs = [(md, 1)] if hek == 7 else []
                r.op("tensor", lambda e, dm=dm, qc=qc, hek=hek, b=b: e.matmul(
                    b.t[:, :], woutb[:, hek * 1024 + dm * 128: hek * 1024 + (dm + 1) * 128],
                    ansT[:, hek * 2048 + qc * 512: hek * 2048 + (qc + 1) * 512],
                    start=(hek == 0), stop=(hek == 7)), incs)
            r.wait("vector", md, md.n)
            r.wait("vector", sd, sd_acc_init)
            r.op("vector", lambda e, dm=dm, qc=qc, b=b: e.tensor_add(
                acc[:, dm * S + qc * 512: dm * S + (qc + 1) * 512],
                acc[:, dm * S + qc * 512: dm * S + (qc + 1) * 512], b.t[:, :]),
                [(vd, 1)])
            b.release(vd)
    r.flush(nc)
    es2.close()

    es3 = contextlib.ExitStack()
    sb3 = lambda name, shape, dtype: es3.enter_context(nc.sbuf_tensor(name, shape, dtype))
    gatb = sb3("gatbs", [128, 8], F32)
    b1c = sb3("b1cs", [128, 128], F32)
    b3c = sb3("b3cs", [128, 128], F32)
    sb1c = sb3("sb1cs", [128, 16], F32)
    sb3c = sb3("sb3cs", [128, 16], F32)
    selm = sb3("selm", [8, 1024], BF16)
    b2c = sb3("b2cs", [8, 1024], BF16)
    gate_ws = sb3("gate_wss", [128, 64], BF16)
    cw_T = sb3("cw_Ts", [8, 2048], BF16)
    w13b = Buf(nc, es3, "w13b", 3, [128, 4096], BF16)
    w2b = Buf(nc, es3, "w2b", 1, [128, 8192], BF16)
    gd_gate_w = dma("gpsimd", gate_ws[:, :], d["gate_ws"], gdg)
    dma("gpsimd", gatb[:, :], d["gate_b"], gdf)
    wsl0 = w13b.acquire(r, "gpsimd")
    j0_ats = [dma("gpsimd", wsl0[:, 0:1024], d["sw13_s"][0][:, 0:1024], w13s[0])]
    for nm, t in [("b1_s", b1c), ("b3_s", b3c),
                  ("sb1_s", sb1c), ("sb3_s", sb3c)]:
        dma("gpsimd", t[:, :], d[nm], gdf)
    for part in range(1, 4):
        j0_ats.append(dma("gpsimd", wsl0[:, part * 1024:(part + 1) * 1024],
                          d["sw13_s"][0][:, part * 1024:(part + 1) * 1024],
                          w13s[part]))
    staged_j0 = (wsl0, j0_ats, 0)
    dma("gpsimd", b2c[0:8, :], d["b2_s"], gdf)
    dma("gpsimd", selm[0:8, :], d_sel, gdf)
    gd_fconsts = gdf.n
    r.wait("tensor", gdg, gd_gate_w)
    r.wait("vector", gdf, gd_fconsts)
    lg_sb = sb3("lg_sb", [8, 512], F32)
    lg_tm = sb3("lg_tm", [128, 128], F32)
    em = sb3("em", [128, 128], F32)
    sct = sb3("sct", [128, 128], F32)
    mk1 = sb3("mk1", [128, 128], F32)
    msk = sb3("msk", [128, 128], F32)
    mk2 = sb3("mk2", [128, 128], F32)
    cwa = sb3("cwa", [128, 128], F32)
    gs1 = sb3("gs1", [128, 16], F32)
    gs2 = sb3("gs2", [128, 16], F32)
    gm1 = sb3("gm1", [128, 16], F32)
    gm2 = sb3("gm2", [128, 16], F32)
    a_sb = sb3("a_sb", [128, 8 * 2048], BF16)
    cwb = sb3("cwb", [128, 2048], BF16)
    pa = [sb3(f"pa{i}", [128, 512], F32) for i in range(2)]
    pbx = [sb3(f"pbx{i}", [128, 512], F32) for i in range(2)]
    mm_ = [sb3(f"mm{i}", [128, 512], F32) for i in range(2)]
    lp_ = [sb3(f"lp{i}", [128, 512], F32) for i in range(2)]
    aa_s = sb3("aa_s", [128, 512], BF16)
    b6 = banks[6]
    b6.acquire(r)
    for qc in range(4):
        b = banks[4 + qc % 2]
        b.acquire(r)
        for dk in range(ND):
            incs = [(md, 1)] if dk == ND - 1 else []
            r.op("tensor", lambda e, dk=dk, qc=qc, b=b: e.matmul(
                b.t[0:8, :], gate_ws[:, dk * 8:(dk + 1) * 8],
                xnf[:, dk * S + qc * 512: dk * S + qc * 512 + 512],
                start=(dk == 0), stop=(dk == ND - 1)), incs)
        r.wait("vector", md, md.n)
        r.op("vector", lambda e, b=b: e.tensor_copy(
            lg_sb[0:8, 0:512], b.t[0:8, :]), [(vd, 1)])
        b.release(vd)
        r.wait("tensor", vd, vd.n)
        for k in range(4):
            ti = qc * 4 + k
            incs = [(md, 1)] if k == 3 else []
            r.op("tensor", lambda e, ti=ti, k=k: e.transpose(
                b6.t[:, ti * 8:(ti + 1) * 8], lg_sb[0:8, k * 128:(k + 1) * 128],
                identf[0:8, 0:8]), incs)
        r.wait("vector", md, md.n)
    r.op("vector", lambda e: e.tensor_copy(lg_tm[:, 0:128], b6.t[:, 0:128]),
         [(vd, 1)])
    b6.release(vd)
    r.wait("scalar", vd, vd.n)
    r.op("scalar", lambda e: e.activation(em[:, :], lg_tm[:, :], AF.Exp), [(sd, 1)])
    r.wait("vector", sd, sd.n)
    for ti in range(NT):
        r.op("vector", lambda e, ti=ti: e.reduce_sum(
            gs1[:, ti:ti + 1], em[:, ti * 8:(ti + 1) * 8], axis=AX))
    r.raw("vector", lambda e: e.drain())
    r.op("vector", lambda e: e.reciprocal(gs2[:, :], gs1[:, :]))
    r.raw("vector", lambda e: e.drain())
    for ti in range(NT):
        r.op("vector", lambda e, ti=ti: e.tensor_scalar(
            sct[:, ti * 8:(ti + 1) * 8], em[:, ti * 8:(ti + 1) * 8],
            gs2[:, ti:ti + 1], None, ALU.mult))
    r.raw("vector", lambda e: e.drain())
    bia = lg_tm
    for ti in range(NT):
        r.op("vector", lambda e, ti=ti: e.tensor_add(
            bia[:, ti * 8:(ti + 1) * 8], sct[:, ti * 8:(ti + 1) * 8], gatb[:, :]))
    r.raw("vector", lambda e: e.drain())
    for ti in range(NT):
        r.op("vector", lambda e, ti=ti: e.reduce_max(
            gm1[:, ti:ti + 1], bia[:, ti * 8:(ti + 1) * 8], axis=AX))
    r.raw("vector", lambda e: e.drain())
    for ti in range(NT):
        r.op("vector", lambda e, ti=ti: e.tensor_scalar(
            mk1[:, ti * 8:(ti + 1) * 8], bia[:, ti * 8:(ti + 1) * 8],
            gm1[:, ti:ti + 1], None, ALU.is_ge))
    r.raw("vector", lambda e: e.drain())
    for ti in range(NT):
        r.op("vector", lambda e, ti=ti: e.scalar_tensor_tensor(
            msk[:, ti * 8:(ti + 1) * 8], mk1[:, ti * 8:(ti + 1) * 8], -1e9,
            bia[:, ti * 8:(ti + 1) * 8], ALU.mult, ALU.add))
    r.raw("vector", lambda e: e.drain())
    for ti in range(NT):
        r.op("vector", lambda e, ti=ti: e.reduce_max(
            gm2[:, ti:ti + 1], msk[:, ti * 8:(ti + 1) * 8], axis=AX))
    r.raw("vector", lambda e: e.drain())
    for ti in range(NT):
        r.op("vector", lambda e, ti=ti: e.tensor_scalar(
            mk2[:, ti * 8:(ti + 1) * 8], msk[:, ti * 8:(ti + 1) * 8],
            gm2[:, ti:ti + 1], None, ALU.is_ge))
    r.raw("vector", lambda e: e.drain())
    for ti in range(NT):
        r.op("vector", lambda e, ti=ti: e.tensor_add(
            mk1[:, ti * 8:(ti + 1) * 8], mk1[:, ti * 8:(ti + 1) * 8],
            mk2[:, ti * 8:(ti + 1) * 8]))
    r.raw("vector", lambda e: e.drain())
    for ti in range(NT):
        incs = [(vd, 1)] if ti == NT - 1 else []
        r.op("vector", lambda e, ti=ti: e.tensor_mul(
            cwa[:, ti * 8:(ti + 1) * 8], sct[:, ti * 8:(ti + 1) * 8],
            mk1[:, ti * 8:(ti + 1) * 8]), incs)
    vd_cwt_box = [0]

    def emit_cwT():
        r.wait("tensor", vd, vd.n)
        for grp in range(4):
            b = banks[4 + grp % 2]
            b.acquire(r)
            for k in range(4):
                ti = grp * 4 + k
                incs = [(md, 1)] if k == 3 else []
                r.op("tensor", lambda e, ti=ti, k=k, b=b: e.transpose(
                    b.t[0:8, k * 128:(k + 1) * 128], cwa[:, ti * 8:(ti + 1) * 8],
                    identf[:, :]), incs)
            r.wait("vector", md, md.n)
            r.op("vector", lambda e, grp=grp, b=b: e.tensor_copy(
                cw_T[0:8, grp * 512:(grp + 1) * 512], b.t[0:8, 0:512]), [(vd, 1)])
            b.release(vd)
        vd_cwt_box[0] = vd.n

    bank_sets = [(0, 1, 2, 3), (4, 5, 0, 1), (2, 3, 4, 5)]
    gidx = [0]
    last_ap_vd = [0, 0, 0, 0]
    pre_ap_vd = [0, 0, 0, 0]
    pend = []
    sdml_hist = []

    def emit_hgroup(e_idx, j, qc, shared):
        i = gidx[0]
        gidx[0] += 1
        par = i % 2
        st = bank_sets[i % 3]
        bt = [banks[st[0]], banks[st[1]], banks[st[2]], banks[st[3]]]
        wsl, w13_ats, w13_sl = cur_w13[0]
        b1col = (sb1c if shared else b1c)[:, (0 if shared else e_idx) * 16 + j: (0 if shared else e_idx) * 16 + j + 1]
        b3col = (sb3c if shared else b3c)[:, (0 if shared else e_idx) * 16 + j: (0 if shared else e_idx) * 16 + j + 1]
        b1colB = (sb1c if shared else b1c)[:, (0 if shared else e_idx) * 16 + 8 + j: (0 if shared else e_idx) * 16 + 8 + j + 1]
        b3colB = (sb3c if shared else b3c)[:, (0 if shared else e_idx) * 16 + 8 + j: (0 if shared else e_idx) * 16 + 8 + j + 1]
        for src in range(4):
            b = bt[src]
            r.wait("tensor", w13s[w13_sl * 4 + src], w13_ats[src])
            b.acquire(r)
            for dk in range(ND):
                incs = [(md, 1)] if src == 3 and dk == ND - 1 else []
                r.op("tensor", lambda e, src=src, dk=dk, qc=qc, b=b, wsl=wsl: e.matmul(
                    b.t[:, :], wsl[:, (src * 8 + dk) * 128:(src * 8 + dk + 1) * 128],
                    xnf[:, dk * S + qc * 512: dk * S + qc * 512 + 512],
                    start=(dk == 0), stop=(dk == ND - 1)), incs)
        md_h = md.n
        if qc == 3:
            w13b.release(md)
        r.wait("vector", md, md_h)
        if len(sdml_hist) >= 2:
            r.wait("vector", sd, sdml_hist[-2])
        r.op("vector", lambda e, par=par, bt=bt, b3col=b3col: e.tensor_scalar(
            pa[par][:, :], bt[1].t[:, :], b3col, None, ALU.add), [(vd, 1)])
        bt[1].release(vd)
        r.op("vector", lambda e, par=par, bt=bt, b3colB=b3colB: e.tensor_scalar(
            pbx[par][:, :], bt[3].t[:, :], b3colB, None, ALU.add), [(vd, 1)])
        bt[3].release(vd)
        r.raw("vector", lambda e: e.drain())
        r.op("vector", lambda e, par=par, bt=bt, b1col=b1col: e.scalar_tensor_tensor(
            pa[par][:, :], bt[0].t[:, :], b1col, pa[par][:, :], ALU.add, ALU.mult),
            [(vd, 1)])
        bt[0].release(vd)
        r.op("vector", lambda e, par=par, bt=bt, b1colB=b1colB: e.scalar_tensor_tensor(
            pbx[par][:, :], bt[2].t[:, :], b1colB, pbx[par][:, :], ALU.add, ALU.mult),
            [(vd, 1)])
        bt[2].release(vd)
        r.raw("vector", lambda e: e.drain())
        if pend:
            pend.pop(0)(0)
        r.op("vector", lambda e, par=par: e.tensor_scalar(
            pa[par][:, :], pa[par][:, :], 7.0, None, ALU.min), [(vd, 1)])
        vd_g = vd.n
        r.raw("vector", lambda e: e.drain())
        if pend:
            pend.pop(0)(0)
        r.op("vector", lambda e, par=par: e.tensor_scalar(
            pbx[par][:, :], pbx[par][:, :], 7.0, -7.0, ALU.min, ALU.max), [(vd, 1)])
        vd_u = vd.n
        r.wait("scalar", vd, vd_g)
        r.op("scalar", lambda e, par=par: e.activation(
            mm_[par][:, :], pa[par][:, :], AF.Silu, scale=1.702))
        r.wait("scalar", vd, vd_u)
        r.op("scalar", lambda e, par=par: e.activation(
            lp_[par][:, :], pbx[par][:, :], AF.Identity, scale=1.0 / 1.702,
            bias=cInv[:, 0:1]), [(sd, 1)])
        sd_ml = sd.n
        sdml_hist.append(sd_ml)
        dst = a_sb[:, j * 2048 + qc * 512: j * 2048 + (qc + 1) * 512]

        def fin_aa(_=0, par=par, dst=dst, sd_ml=sd_ml, shared=shared, qc=qc, j=j):
            r.wait("vector", sd, sd_ml)
            if shared:
                r.op("vector", lambda e: e.tensor_mul(dst, mm_[par][:, :], lp_[par][:, :]),
                     [(vd, 1)])
                if j == 7:
                    last_ap_vd[qc] = vd.n
                elif j == 6:
                    pre_ap_vd[qc] = vd.n
            else:
                r.op("vector", lambda e: e.tensor_mul(aa_s[:, :], mm_[par][:, :], lp_[par][:, :]))

        def fin_ap(_=0, dst=dst, shared=shared, qc=qc, j=j):
            if not shared:
                r.op("vector", lambda e: e.tensor_mul(
                    dst, aa_s[:, :], cwb[:, qc * 512:(qc + 1) * 512]), [(vd, 1)])
                if j == 7:
                    last_ap_vd[qc] = vd.n
                elif j == 6:
                    pre_ap_vd[qc] = vd.n

        pend.append(fin_aa)
        pend.append(fin_ap)

    cur_w13 = [None]
    w13cnt = [0]

    def emit_expert(e_idx):
        shared = e_idx == EXP
        if not shared:
            r.wait("tensor", vd, vd_cwt_box[0])
            for qc in range(4):
                b = banks[6 + qc % 2]
                b.acquire(r)
                incs = [(md, 1)]
                r.op("tensor", lambda e, e_idx=e_idx, qc=qc, b=b: e.matmul(
                    b.t[:, :], selm[0:8, e_idx * 128:(e_idx + 1) * 128],
                    cw_T[0:8, qc * 512:(qc + 1) * 512], start=True, stop=True), incs)
                r.wait("vector", md, md.n)
                r.op("vector", lambda e, qc=qc, b=b: e.tensor_copy(
                    cwb[:, qc * 512:(qc + 1) * 512], b.t[:, :]), [(vd, 1)])
                b.release(vd)
            r.raw("vector", lambda e: e.drain())
        slabs = []

        def issue_w13(j):
            wsl = w13b.acquire(r, "gpsimd")
            i = w13cnt[0]
            w13cnt[0] += 1
            sl = i % 3
            src_ap = d["sw13_s"][j] if shared else d["w13_s"][e_idx, j]
            ats = []
            for part in range(4):
                ats.append(dma(
                    "gpsimd", wsl[:, part * 1024:(part + 1) * 1024],
                    src_ap[:, part * 1024:(part + 1) * 1024], w13s[sl * 4 + part]))
            slabs.append((wsl, ats, sl))

        if shared:
            slabs.append(staged_j0)
            w13cnt[0] += 1
        else:
            issue_w13(0)
        issue_w13(1)
        issue_w13(2)
        wv = w2b.acquire(r, "gpsimd")
        gd_w2 = dma("gpsimd", wv[:, :], d["sw2_s"] if shared else d["w2_s"][e_idx], w2s)
        for j in range(8):
            if j >= 3:
                issue_w13(j)
            cur_w13[0] = slabs[j]
            for qc in range(4):
                emit_hgroup(e_idx, j, qc, shared)
        while pend:
            pend.pop(0)(0)
            r.raw("vector", lambda e: e.drain())
        r.wait("tensor", w2s, gd_w2)

        def w2_mms(qc, dm, b, fks):
            first = fks[0] == 0
            if e_idx == 0 and first:
                r.op("tensor", lambda e, dm=dm, qc=qc, b=b: e.matmul(
                    b.t[:, :], b2c[0:8, dm * 128:(dm + 1) * 128],
                    cw_T[0:8, qc * 512:(qc + 1) * 512], start=True, stop=False))
                first = False
            for fk in fks:
                incs = [(md, 1)] if fk == 7 else []
                r.op("tensor", lambda e, dm=dm, qc=qc, fk=fk, b=b, wv=wv,
                     st=(first and fk == fks[0]): e.matmul(
                    b.t[:, :], wv[:, fk * 1024 + dm * 128: fk * 1024 + (dm + 1) * 128],
                    a_sb[:, fk * 2048 + qc * 512: fk * 2048 + (qc + 1) * 512],
                    start=st, stop=(fk == 7)), incs)

        def w2_evac(qc, dm, b):
            r.wait("vector", md, md.n)
            r.op("vector", lambda e, dm=dm, qc=qc, b=b: e.tensor_add(
                acc[:, dm * S + qc * 512: dm * S + (qc + 1) * 512],
                acc[:, dm * S + qc * 512: dm * S + (qc + 1) * 512], b.t[:, :]),
                [(vd, 1)])
            b.release(vd)

        s1 = bank_sets[(gidx[0] - 2) % 3]
        s2 = bank_sets[(gidx[0] - 1) % 3]
        extra = [bi for bi in s1 if bi not in s2][:2]
        pre = [banks[6], banks[7], banks[extra[0]], banks[extra[1]]]
        r.wait("tensor", vd, pre_ap_vd[0])
        for dm in range(4):
            pre[dm].acquire(r)
            w2_mms(0, dm, pre[dm], list(range(7)))
        r.wait("tensor", vd, last_ap_vd[0])
        for dm in range(4):
            w2_mms(0, dm, pre[dm], [7])
            w2_evac(0, dm, pre[dm])
        for qc in range(4):
            r.wait("tensor", vd, last_ap_vd[qc])
            for dm in range(4 if qc == 0 else 0, ND):
                b = banks[6 + (qc * 8 + dm) % 2]
                b.acquire(r)
                w2_mms(qc, dm, b, list(range(8)))
                if dm == ND - 1 and qc == 3:
                    w2b.release(md)
                w2_evac(qc, dm, b)

    emit_expert(EXP)
    r.wait("tensor", gdf, gd_fconsts)
    emit_cwT()
    for e_idx in range(EXP):
        emit_expert(e_idx)
    vd_ffn_done = vd.n
    r.flush(nc)
    es3.close()

    es4 = contextlib.ExitStack()
    otb = Buf(nc, es4, "otb", 2, [128, 1024], F32)
    r.wait("tensor", vd, vd_ffn_done)
    for ti in range(NT):
        ot = otb.acquire(r, "vector")
        bA = banks[(2 * ti) % 8]
        bB = banks[(2 * ti + 1) % 8]
        bA.acquire(r)
        bB.acquire(r)
        for dk in range(ND):
            b = bA if dk < 4 else bB
            incs = [(md, 1)] if dk % 4 == 3 else []
            r.op("tensor", lambda e, dk=dk, ti=ti, b=b: e.transpose(
                b.t[:, (dk % 4) * 128:(dk % 4 + 1) * 128],
                acc[:, dk * S + ti * 128: dk * S + (ti + 1) * 128],
                identf[:, :]), incs)
        r.wait("vector", md, md.n)
        r.op("vector", lambda e, ot=ot, bA=bA: e.tensor_copy(
            ot[:, 0:512], bA.t[:, :]), [(vd, 1)])
        bA.release(vd)
        r.op("vector", lambda e, ot=ot, bB=bB: e.tensor_copy(
            ot[:, 512:1024], bB.t[:, :]), [(vd, 1)])
        bB.release(vd)
        r.wait("sync", vd, vd.n)
        dma("sync", out_hbm[ti * 128:(ti + 1) * 128, :], ot[:, :], ods[ti % 2])
        otb.release(ods[ti % 2])
    r.wait("sync", ods[0], ods[0].n)
    r.wait("sync", ods[1], ods[1].n)
    r.flush(nc)
    es4.close()
    esP.close()
    es0.close()
    return nc


def kernel(**inputs):
    x_all = np.asarray(inputs["L0_attn_x"], dtype=np.float32)
    lay = host_layouts(inputs)
    nc = build_nc()
    in_maps = []
    for c in range(8):
        m = {"x": np.ascontiguousarray(x_all[c])}
        m.update(lay)
        in_maps.append(m)
    res = run_bass_kernel_spmd(nc, in_maps, core_ids=list(range(8)))
    return np.stack([np.asarray(r["out"], dtype=np.float32) for r in res.results], axis=0)


if __name__ == "__main__":
    import reference as _ref
    inp = _ref.setup_inputs()
    out = kernel(**{k: np.asarray(v) for k, v in inp.items()})
    exp = np.asarray(_ref.reference(**inp))
    rel = np.linalg.norm(out - exp) / np.linalg.norm(exp)
    print("rel:", rel)


# version 24
# speedup vs baseline: 1.0225x; 1.0065x over previous
import contextlib
import numpy as np
import ml_dtypes
import concourse.bass as bass
import concourse.mybir as mybir
from concourse.bass_utils import run_bass_kernel_spmd

F32 = mybir.dt.float32
BF16 = mybir.dt.bfloat16
AF = mybir.ActivationFunctionType
ALU = mybir.AluOpType
AX = mybir.AxisListType.X
BFD = ml_dtypes.bfloat16

S = 2048
DMODEL = 1024
NT = 16
ND = 8
NP = 8
EXP = 8
CSHIFT = 12.0

ENGS = ("sync", "gpsimd", "tensor", "vector", "scalar")


class Sem:
    def __init__(self, nc, es, name, owner=None):
        self.h = es.enter_context(nc.semaphore(name))
        self.n = 0
        self.owner = owner


class Rec:
    def __init__(self):
        self.ops = {e: [] for e in ENGS}
        self.last_wait = {}

    def op(self, eng, fn, incs=()):
        incs = tuple(incs)
        main, extra = incs[:2], incs[2:]
        if extra:
            assert eng in ("vector", "scalar", "gpsimd"), "no engine_nop overflow on " + eng

        def f(e, fn=fn, main=main, extra=extra):
            i = fn(e)
            for s, k in main:
                i.then_inc(s.h, k)
            while extra:
                chunk, extra = extra[:2], extra[2:]
                j = e.engine_nop()
                for s, k in chunk:
                    j.then_inc(s.h, k)
        self.ops[eng].append(f)
        for s, k in incs:
            s.n += k

    def wait(self, eng, sem, val=None):
        if val is None:
            val = sem.n
        if val <= 0 or sem.owner == eng:
            return
        key = (eng, id(sem))
        if self.last_wait.get(key, -1) >= val:
            return
        self.last_wait[key] = val
        self.ops[eng].append(lambda e, s=sem, v=val: e.wait_ge(s.h, v))

    def raw(self, eng, fn):
        self.ops[eng].append(fn)

    def flush(self, nc):
        with nc.Block() as block:
            for name in ENGS:
                ops = self.ops[name]
                if ops:
                    getattr(block, name)(lambda e, ops=list(ops): [f(e) for f in ops])
        self.ops = {e: [] for e in ENGS}


class Rel:

    def __init__(self):
        self.evts = []
        self.acq = 0

    def acquire(self, r, eng):
        assert len(self.evts) >= self.acq, "acquire before prior release recorded"
        if self.acq > 0:
            sem, val = self.evts[self.acq - 1]
            r.wait(eng, sem, val)
        self.acq += 1

    def release(self, sem, val=None):
        self.evts.append((sem, val if val is not None else sem.n))


class Bank(Rel):
    def __init__(self, nc, es, idx):
        super().__init__()
        self.t = es.enter_context(nc.psum_tensor(f"pbank{idx}", [128, 512], F32))

    def acquire(self, r, eng="tensor"):
        super().acquire(r, eng)


class SubBank(Rel):
    def __init__(self, region, lo):
        super().__init__()
        self.t = region[:, lo:lo + 512]

    def acquire(self, r, eng="tensor"):
        super().acquire(r, eng)


class Buf(Rel):
    def __init__(self, nc, es, name, n, shape, dtype):
        super().__init__()
        self.ts = [es.enter_context(nc.sbuf_tensor(f"{name}{i}", shape, dtype))
                   for i in range(n)]
        self.nb = n

    def acquire(self, r, eng):
        u = self.acq
        if u >= self.nb:
            sem, val = self.evts[u - self.nb]
            r.wait(eng, sem, val)
        self.acq += 1
        return self.ts[u % self.nb]

    def reserve(self):
        u = self.acq
        w = self.evts[u - self.nb] if u >= self.nb else None
        self.acq += 1
        return self.ts[u % self.nb], w


def host_layouts(inp):
    f32 = np.float32
    a = {k: np.asarray(v, dtype=f32) for k, v in inp.items()}
    out = {}

    wq = a["L0_attn_w_qkv"].reshape(1024, 3072)
    out["wqkv_s"] = np.ascontiguousarray(
        wq.reshape(8, 128, 3, 8, 128).transpose(3, 1, 2, 0, 4)).reshape(8, 128, 3072)
    bq = a["L0_attn_b_qkv"].reshape(3072)
    out["bqkT"] = np.ascontiguousarray(bq.reshape(24, 128).T)
    out["brow_v"] = np.ascontiguousarray(bq[2048:].reshape(1, 1024))
    out["wout_s"] = np.ascontiguousarray(a["L0_attn_w_out"].reshape(1024, 1024))
    out["bout_s"] = np.ascontiguousarray(a["L0_attn_b_out"].reshape(8, 128).T)

    out["gate_ws"] = np.ascontiguousarray(
        a["L0_ffn_gate_w"].T.reshape(8, 128, 8).transpose(1, 0, 2).reshape(128, 64))
    out["gate_b"] = np.ascontiguousarray(np.broadcast_to(a["L0_ffn_gate_b"], (128, 8)))

    perm = np.concatenate([np.arange(0, 2048, 2), np.arange(1, 2048, 2)])

    def prep13(w):
        wp = w[..., perm, :]
        wt = np.swapaxes(wp, -1, -2)
        lead = wt.shape[:-2]
        t = wt.reshape(lead + (8, 128, 16, 128))
        t = np.moveaxis(t, -2, -4)
        t = np.swapaxes(t, -3, -2)
        return t

    def w13_fused(w1, w3):
        t1, t3 = prep13(w1), prep13(w3)
        A1, B1 = t1[..., 0:8, :, :, :], t1[..., 8:16, :, :, :]
        A3, B3 = t3[..., 0:8, :, :, :], t3[..., 8:16, :, :, :]
        st = np.stack([A1, A3, B1, B3], axis=-3)
        lead = st.shape[:-4]
        return np.ascontiguousarray(st.reshape(lead + (128, 4096)))

    out["w13_s"] = w13_fused(a["L0_ffn_w1"], a["L0_ffn_w3"])
    out["sw13_s"] = w13_fused(a["L0_ffn_sw1"][None], a["L0_ffn_sw3"][None])[0]

    def bperm(b):
        bp = b[..., perm]
        lead = bp.shape[:-1]
        t = bp.reshape(lead + (16, 128))
        t = np.moveaxis(t, -1, 0)
        return np.ascontiguousarray(t.reshape((128, -1)))
    out["b1_s"] = bperm(a["L0_ffn_b1"])
    out["b3_s"] = bperm(a["L0_ffn_b3"])
    out["sb1_s"] = bperm(a["L0_ffn_sb1"])
    out["sb3_s"] = bperm(a["L0_ffn_sb3"])

    def w2_slab(w2):
        wt = np.swapaxes(w2, -1, -2)
        lead = wt.shape[:-2]
        t = wt.reshape(lead + (8, 128, 1024)).swapaxes(-3, -2)
        return np.ascontiguousarray(t.reshape(lead + (128, 8192)))
    out["w2_s"] = w2_slab(a["L0_ffn_w2"])
    out["sw2_s"] = w2_slab(a["L0_ffn_sw2"][None])[0]
    out["b2_s"] = np.ascontiguousarray(a["L0_ffn_b2"])
    out["sb2_s"] = np.ascontiguousarray(a["L0_ffn_sb2"].reshape(8, 128).T)
    return out


def build_nc():
    nc = bass.Bass("TRN2", target_bir_lowering=False)
    r = Rec()

    d = {}
    for k, shape in {
        "x": [S, DMODEL], "bqkT": [128, 24], "bout_s": [128, 8], "sb2_s": [128, 8],
        "gate_b": [128, 8], "b1_s": [128, 128], "b3_s": [128, 128],
        "sb1_s": [128, 16], "sb3_s": [128, 16],
        "wqkv_s": [8, 128, 3072], "brow_v": [1, 1024], "wout_s": [1024, 1024],
        "gate_ws": [128, 64], "w13_s": [EXP, 8, 128, 4096], "w2_s": [EXP, 128, 8192],
        "sw13_s": [8, 128, 4096], "sw2_s": [128, 8192], "b2_s": [8, 1024],
    }.items():
        d[k] = nc.dram_tensor(k, shape, F32, kind="ExternalInput").ap()
    out_hbm = nc.dram_tensor("out", [S, DMODEL], F32, kind="ExternalOutput").ap()

    tri_np = np.triu(np.ones((128, 128), dtype=BFD))
    amask_np = ((1.0 - np.triu(np.ones((128, 128)))) * -3e9).astype(BFD)
    identb_np = np.eye(128, dtype=BFD)
    identf_np = np.eye(128, dtype=np.float32)
    sel_np = np.zeros((8, 8 * 128), dtype=BFD)
    for e in range(8):
        sel_np[e, e * 128:(e + 1) * 128] = 1
    sel2_np = np.zeros((33, 128), dtype=BFD)
    sel2_np[0, 0:64] = 1
    sel2_np[32, 64:128] = 1
    d_sel2 = nc.inline_tensor(sel2_np, name="sel2_c").ap()
    d_tri = nc.inline_tensor(tri_np, name="tri_c").ap()
    d_amask = nc.inline_tensor(amask_np, name="amask_c").ap()
    d_idb = nc.inline_tensor(identb_np, name="identb_c").ap()
    d_idf = nc.inline_tensor(identf_np, name="identf_c").ap()
    d_sel = nc.inline_tensor(sel_np, name="sel_c").ap()

    es0 = contextlib.ExitStack()
    sb0 = lambda name, shape, dtype: es0.enter_context(nc.sbuf_tensor(name, shape, dtype))

    xnf = sb0("xnf", [128, ND * S], BF16)
    acc = sb0("acc", [128, ND * S], F32)
    bqkT = sb0("bqkTs", [128, 24], F32)
    bout = sb0("bouts", [128, 8], F32)
    sb2c = sb0("sb2cs", [128, 8], F32)
    binit = sb0("binit", [128, 8], F32)
    negC = sb0("negC", [128, 1], F32)
    cInv = sb0("cInv", [128, 1], F32)
    identf = sb0("identf", [128, 128], F32)
    identb = sb0("identb", [128, 128], BF16)
    tri = sb0("tri", [128, 128], BF16)
    amask = sb0("amask", [128, 128], BF16)
    ones_bf = sb0("ones_bf", [1, 128], BF16)
    brow_v = sb0("brow_vs", [1, 1024], BF16)

    sx = Sem(nc, es0, "sx", "sync")
    gdc = Sem(nc, es0, "gdc", "gpsimd")
    gdw = Sem(nc, es0, "gdw", "gpsimd")
    gdf = Sem(nc, es0, "gdf", "gpsimd")
    gdg = Sem(nc, es0, "gdg", "gpsimd")
    wqs = [Sem(nc, es0, f"wqs{i}", "gpsimd") for i in range(6)]
    w13s = [Sem(nc, es0, f"w13s{i}", "gpsimd") for i in range(12)]
    w2s = Sem(nc, es0, "w2s", "gpsimd")
    ods = [Sem(nc, es0, f"ods{i}", "sync") for i in range(2)]
    sxs_ = [Sem(nc, es0, f"sxc{i}", "sync") for i in range(4)]
    md = Sem(nc, es0, "md", "tensor")
    vd = Sem(nc, es0, "vd", "vector")
    sd = Sem(nc, es0, "sd", "scalar")

    def dma(q, dst, src, sem):
        r.op(q, lambda e, dst=dst, src=src: e.dma_start(dst, src), [(sem, 16)])
        return sem.n

    for nm, t in [("bqkT", bqkT), ("bout_s", bout), ("sb2_s", sb2c)]:
        dma("gpsimd", t[:, :], d[nm], gdc)
    dma("gpsimd", brow_v[0:1, :], d["brow_v"], gdc)
    dma("gpsimd", tri[:, :], d_tri, gdc)
    dma("gpsimd", amask[:, :], d_amask, gdc)
    dma("gpsimd", identf[:, :], d_idf, gdc)
    dma("gpsimd", identb[:, :], d_idb, gdc)
    gd_consts = gdc.n

    r.op("vector", lambda e: e.memset(ones_bf[0:1, :], 1.0))
    r.op("vector", lambda e: e.memset(negC[:, :], -float(CSHIFT)))
    r.op("vector", lambda e: e.memset(cInv[:, :], 1.0 / 1.702), [(vd, 1)])
    vd_consts = vd.n
    r.wait("vector", gdc, gd_consts)
    r.op("vector", lambda e: e.tensor_add(binit[:, :], bout[:, :], sb2c[:, :]), [(vd, 1)])
    vd_binit = vd.n

    es1 = contextlib.ExitStack()
    sb1_ = lambda name, shape, dtype: es1.enter_context(nc.sbuf_tensor(name, shape, dtype))
    x_tm = sb1_("x_tm", [128, NT * 1024], F32)
    xn_tm = Buf(nc, es1, "xn_tm", 2, [128, 1024], BF16)
    bst = sb1_("bst", [128, NT * 12], F32)
    mv = sb1_("mv", [128, NT * 2], F32)
    stv = sb1_("stv", [128, 16], F32)
    str_ = sb1_("str", [128, 16], F32)
    stn = sb1_("stn", [128, 16], F32)
    pb_t = [es1.enter_context(nc.psum_tensor(f"pbf{i}", [128, 1024], BF16)) for i in range(2)]
    pf_t = [es1.enter_context(nc.psum_tensor(f"pff{i}", [128, 512], F32)) for i in range(4)]
    pbR = [Rel(), Rel()]
    pfR = [Rel(), Rel(), Rel(), Rel()]

    for ch in range(4):
        dma("sync",
            x_tm[:, ch * 4096:(ch + 1) * 4096].rearrange("p (t c) -> p t c", t=4),
            d["x"][ch * 512:(ch + 1) * 512, :].rearrange("(t p) c -> p t c", p=128),
            sxs_[ch])
    for ti in range(NT):
        r.wait("vector", sxs_[ti // 4], 16)
        r.op("vector", lambda e, ti=ti: e.bn_stats(
            bst[:, ti * 12: ti * 12 + 6], x_tm[:, ti * 1024: ti * 1024 + 512]))
        r.op("vector", lambda e, ti=ti: e.bn_stats(
            bst[:, ti * 12 + 6: ti * 12 + 12], x_tm[:, ti * 1024 + 512: (ti + 1) * 1024]))
    r.raw("vector", lambda e: e.drain())
    for ti in range(NT):
        r.op("vector", lambda e, ti=ti: e.bn_aggr(
            mv[:, ti * 2: ti * 2 + 2], bst[:, ti * 12: (ti + 1) * 12]))
    r.raw("vector", lambda e: e.drain())
    mv3 = mv[:, :].rearrange("p (t two) -> p t two", two=2)
    r.op("vector", lambda e: e.tensor_scalar(
        stv[:, :].rearrange("p (t one) -> p t one", one=1), mv3[:, :, 1:2],
        1e-5, None, ALU.add), [(vd, 1)])
    r.wait("scalar", vd, vd.n)
    r.op("scalar", lambda e: e.activation(stv[:, :], stv[:, :], AF.Sqrt), [(sd, 1)])
    r.wait("vector", sd, sd.n)
    r.op("vector", lambda e: e.reciprocal(str_[:, :], stv[:, :]))
    r.raw("vector", lambda e: e.drain())
    r.op("vector", lambda e: e.scalar_tensor_tensor(
        stn[:, :].rearrange("p (t one) -> p t one", one=1), mv3[:, :, 0:1], -1.0,
        str_[:, :].rearrange("p (t one) -> p t one", one=1),
        ALU.mult, ALU.mult), [(vd, 1)])
    r.raw("vector", lambda e: e.drain())
    vd_stats = vd.n

    vd_xn_at = [0] * NT
    for ti in range(NT):
        xb = xn_tm.acquire(r, "vector")
        r.wait("vector", vd, vd_stats)
        r.op("vector", lambda e, ti=ti, xb=xb: e.tensor_scalar(
            xb[:, :], x_tm[:, ti * 1024:(ti + 1) * 1024],
            str_[:, ti:ti + 1], stn[:, ti:ti + 1], ALU.mult, ALU.add), [(vd, 1)])
        r.wait("tensor", vd, vd.n)
        pb = pb_t[ti % 2]
        pbR[ti % 2].acquire(r, "tensor")
        r.wait("tensor", gdc, gd_consts)
        for dk in range(ND):
            incs = [(md, 1)] if dk == ND - 1 else []
            r.op("tensor", lambda e, dk=dk, xb=xb, pb=pb: e.transpose(
                pb[:, dk * 128:(dk + 1) * 128], xb[:, dk * 128:(dk + 1) * 128],
                identb[:, :]), incs)
        md_xn = md.n
        xn_tm.release(md)
        pfb = (ti % 2) * 2
        pfR[pfb].acquire(r, "tensor")
        pfR[pfb + 1].acquire(r, "tensor")
        for dk in range(ND):
            incs = [(md, 1)] if dk % 4 == 3 else []
            r.op("tensor", lambda e, dk=dk, ti=ti, pfb=pfb: e.transpose(
                pf_t[pfb + dk // 4][:, (dk % 4) * 128:(dk % 4 + 1) * 128],
                x_tm[:, ti * 1024 + dk * 128: ti * 1024 + (dk + 1) * 128],
                identf[:, :]), incs)
        md_x = md.n
        r.wait("vector", md, md_xn)
        r.op("vector", lambda e, ti=ti, pb=pb: e.tensor_copy(
            xnf[:, :].rearrange("p (dk s) -> p dk s", dk=ND)[:, :, ti * 128:(ti + 1) * 128],
            pb[:, :].rearrange("p (dk c) -> p dk c", dk=ND)), [(vd, 1)])
        vd_xn_at[ti] = vd.n
        pbR[ti % 2].release(vd)
        r.wait("scalar", md, md_x)
        r.wait("scalar", vd, vd_binit)
        for dk in range(ND):
            incs = [(sd, 1)] if dk == 3 or dk == 7 else []
            r.op("scalar", lambda e, dk=dk, ti=ti, pfb=pfb: e.activation(
                acc[:, dk * S + ti * 128: dk * S + (ti + 1) * 128],
                pf_t[pfb + dk // 4][:, (dk % 4) * 128:(dk % 4 + 1) * 128], AF.Identity,
                bias=binit[:, dk:dk + 1]), incs)
            if dk == 3:
                pfR[pfb].release(sd)
            if dk == 7:
                pfR[pfb + 1].release(sd)
    vd_xn_done = vd.n
    sd_acc_init = sd.n
    r.flush(nc)
    es1.close()

    esP = contextlib.ExitStack()
    banks = [Bank(nc, esP, i) for i in range(4)]
    reg45 = esP.enter_context(nc.psum_tensor("reg45", [128, 1024], F32))
    reg67 = esP.enter_context(nc.psum_tensor("reg67", [128, 1024], F32))
    banks = banks + [SubBank(reg45, 0), SubBank(reg45, 512),
                     SubBank(reg67, 0), SubBank(reg67, 512)]
    regions = [(reg45, banks[4], banks[5]), (reg67, banks[6], banks[7])]
    rot_i = [0]

    def rot_next():
        reg = regions[rot_i[0] % 2]
        rot_i[0] += 1
        reg[1].acquire(r)
        reg[2].acquire(r)
        return reg

    def rot_release(reg, sem):
        reg[1].release(sem)
        reg[2].release(sem)

    es2 = contextlib.ExitStack()
    wqkvb = Buf(nc, es2, "wqkvb", 2, [128, 3072], BF16)
    qkb = Buf(nc, es2, "qkb", 2, [128, 4096], BF16)
    vhb = Buf(nc, es2, "vhb", 2, [128, 16 * 160], BF16)
    ptb = Buf(nc, es2, "ptb", 4, [128, 1024], BF16)
    lvf = Buf(nc, es2, "lvf", 2, [128, 512], F32)
    lvb = Buf(nc, es2, "lvb", 2, [64, 512], BF16)
    lvc = Buf(nc, es2, "lvc", 2, [128, 512], F32)
    ansT = es2.enter_context(nc.sbuf_tensor("ansT", [128, 8 * 2048], BF16))
    woutb = es2.enter_context(nc.sbuf_tensor("woutb", [128, 8192], BF16))
    sel2 = es2.enter_context(nc.sbuf_tensor("sel2", [33, 128], BF16))
    gd_sel2 = dma("gpsimd", sel2[0:33, :], d_sel2, gdc)
    for i in range(2):
        r.op("vector", lambda e, i=i: e.memset(lvf.ts[i][0:64, :], 1.0))
    r.raw("vector", lambda e: e.drain())

    for i in range(2):
        r.op("vector", lambda e, i=i: e.memset(
            vhb.ts[i][:, :].rearrange("p (kt z) -> p kt z", z=160)[:, :, 64:65], 1.0))
        r.op("vector", lambda e, i=i: e.memset(
            vhb.ts[i][:, :].rearrange("p (kt z) -> p kt z", z=160)[:, :, 144:145], 1.0),
            [(vd, 1)])
    vd_vones = vd.n
    gd_wout_box = [0]

    pair_sd_done = {}

    def make_qkv_jobs(p):
        wt = wqkvb.acquire(r, "gpsimd")
        qb = (p % 2) * 3
        w_ats = [dma("gpsimd", wt[:, part * 1024:(part + 1) * 1024],
                     d["wqkv_s"][p][:, part * 1024:(part + 1) * 1024],
                     wqs[qb + part]) for part in range(3)]
        qt, qk_w = qkb.reserve()
        vt, vh_w = vhb.reserve()
        jobs = []

        def qk_job(tq, qc, first):
            def f():
                if qc == 0:
                    r.wait("tensor", wqs[qb + tq], w_ats[tq])
                if first:
                    if qk_w:
                        r.wait("scalar", *qk_w)
                r.wait("tensor", vd, vd_xn_at[qc * 4 + 3])
                rr = rot_next()
                for dk in range(ND):
                    incs = [(md, 1)] if dk == ND - 1 else []
                    r.op("tensor", lambda e, tq=tq, dk=dk, qc=qc, rr=rr, wt=wt: e.matmul(
                        rr[0][:, 0:512], wt[:, (tq * 8 + dk) * 128:(tq * 8 + dk + 1) * 128],
                        xnf[:, dk * S + qc * 512: dk * S + qc * 512 + 512],
                        start=(dk == 0), stop=(dk == ND - 1)), incs)
                r.wait("scalar", md, md.n)
                r.op("scalar", lambda e, tq=tq, qc=qc, rr=rr, qt=qt, p=p: e.activation(
                    qt[:, tq * 2048 + qc * 512: tq * 2048 + (qc + 1) * 512], rr[0][:, 0:512],
                    AF.Identity, bias=bqkT[:, tq * 8 + p: tq * 8 + p + 1]), [(sd, 1)])
                rot_release(rr, sd)
            return f

        def v_job(kt, first):
            def f():
                if first:
                    r.wait("tensor", wqs[qb + 2], w_ats[2])
                    r.wait("tensor", vd, vd_vones)
                    if vh_w:
                        r.wait("scalar", *vh_w)
                r.wait("tensor", vd, vd_xn_at[kt])
                rr = rot_next()
                for dk in range(ND):
                    r.op("tensor", lambda e, kt=kt, dk=dk, rr=rr, wt=wt: e.matmul(
                        rr[0][:, 0:128], xnf[:, dk * S + kt * 128: dk * S + (kt + 1) * 128],
                        wt[:, (16 + dk) * 128:(16 + dk + 1) * 128],
                        start=(dk == 0), stop=False))
                r.op("tensor", lambda e, rr=rr, p=p: e.matmul(
                    rr[0][:, 0:128], ones_bf[0:1, 0:128], brow_v[0:1, p * 128:(p + 1) * 128],
                    start=False, stop=True), [(md, 1)])
                r.wait("scalar", md, md.n)
                r.op("scalar", lambda e, kt=kt, rr=rr, vt=vt: e.copy(
                    vt[:, kt * 160:kt * 160 + 160].rearrange("p (h z) -> p h z", z=80)[:, :, 0:64],
                    rr[0][:, 0:128].rearrange("p (h z) -> p h z", z=64)), [(sd, 1)])
                rot_release(rr, sd)
                if kt == NT - 1:
                    wqkvb.release(sd)
                    pair_sd_done[p] = sd.n
            return f

        for tq in (0, 1):
            for qc in range(4):
                jobs.append(qk_job(tq, qc, tq == 0 and qc == 0))
        for kt in range(NT):
            jobs.append(v_job(kt, kt == 0))
        return jobs, qt, vt

    def emit_attn(p, qt, vt, pump):
        r.wait("tensor", sd, pair_sd_done[p])
        for half in (0, 1):
            q0 = half * 1024
            nki = 8 if half == 0 else 16
            exp_at = [[0] * nki for _ in range(2)]
            pt_of = [[None] * nki for _ in range(2)]

            def emit_scores(hl, ki, half=half, q0=q0):
                off = hl * 64
                pt = ptb.acquire(r, "scalar")
                pt_of[hl][ki] = pt
                rr = rot_next()
                reg = rr[0]
                qs0 = max(ki * 128, q0)
                have_diag = ki * 128 >= q0
                diag_hb = (ki * 128 - q0) // 512 if have_diag else -1
                for hb in (0, 1):
                    lo = max(qs0, q0 + 512 * hb)
                    hi = q0 + 512 * (hb + 1)
                    if lo >= hi:
                        continue
                    din = diag_hb == hb
                    lastb = hb == 1 or (hb == 0 and qs0 >= q0 + 512)
                    r.op("tensor", lambda e, ki=ki, lo=lo, n=hi - lo, off=off, q0=q0,
                         reg=reg, din=din: e.matmul(
                        reg[:, lo - q0: lo - q0 + n],
                        qt[off:off + 64, 2048 + ki * 128: 2048 + (ki + 1) * 128],
                        qt[off:off + 64, lo:lo + n], start=True, stop=not din),
                        [(md, 1)] if (not din and hb == 1) else [])
                    if din:
                        r.op("tensor", lambda e, ki=ki, q0=q0, reg=reg: e.matmul(
                            reg[:, ki * 128 - q0: ki * 128 - q0 + 128], identb[:, :],
                            amask[:, :], start=False, stop=True),
                            [(md, 1)] if hb == 1 else [])
                if (diag_hb == 0 and qs0 < q0 + 512) or False:
                    pass
                last_sc = hl == 1 and half == 1 and ki == NT - 1
                if last_sc:
                    qkb.release(md)
                r.wait("scalar", md, md.n)
                r.op("scalar", lambda e, qs0=qs0, q0=q0, pt=pt, reg=reg: e.activation(
                    pt[:, qs0 - q0: 1024], reg[:, qs0 - q0: 1024], AF.Exp, scale=0.125,
                    bias=negC[:, 0:1]), [(sd, 1)])
                rot_release(rr, sd)
                exp_at[hl][ki] = sd.n

            def emit_av(hl, ki, half=half, q0=q0):
                off = hl * 64
                slot = hl * 80
                if ki == 0:
                    while evac_pend:
                        evac_pend.pop(0)[1]()
                    for lqc in range(2):
                        banks[hl * 2 + lqc].acquire(r)
                pt = pt_of[hl][ki]
                r.wait("tensor", sd, exp_at[hl][ki])
                live = [lqc for lqc in range(2)
                        if ki * 128 < (half * 2 + lqc + 1) * 512]
                for lqc in live:
                    qc = half * 2 + lqc
                    cs = max(0, ki * 128 - qc * 512)
                    n = 512 - cs
                    last_of_qc = ki == min(NT - 1, 4 * qc + 3)
                    incs = [(md, 1)] if lqc == live[-1] else []
                    r.op("tensor", lambda e, hl=hl, lqc=lqc, qc=qc, ki=ki, cs=cs, n=n,
                         pt=pt, slot=slot, q0=q0, last=last_of_qc: e.matmul(
                        banks[hl * 2 + lqc].t[0:65, cs:cs + n],
                        vt[:, ki * 160 + slot: ki * 160 + slot + 65],
                        pt[:, qc * 512 + cs - q0: qc * 512 + cs - q0 + n],
                        start=(ki == 0), stop=last), incs)
                ptb.release(md)

            def emit_evac(lqc, half=half):
                qc = half * 2 + lqc
                bk0 = banks[lqc]
                bk1 = banks[2 + lqc]
                lf = lvf.acquire(r, "vector")
                lb = lvb.acquire(r, "vector")
                r.wait("vector", md, md.n)
                r.op("vector", lambda e, bk0=bk0, lf=lf: e.tensor_copy(
                    lf[0:1, :], bk0.t[64:65, 0:512]))
                r.op("vector", lambda e, bk1=bk1, lf=lf: e.tensor_copy(
                    lf[32:33, :], bk1.t[64:65, 0:512]))
                r.raw("vector", lambda e: e.drain())

                def _recip(e, lf=lf, lb=lb):
                    with nc.allow_low_precision(reason="bf16 store of recip"):
                        e.reciprocal(lb[0:64, :], lf[0:64, :])
                r.op("vector", _recip)
                r.raw("vector", lambda e: e.drain())
                r.op("vector", lambda e: e.engine_nop(), [(vd, 1)])
                lvf.release(vd)
                vd_lb = vd.n

                def rest(qc=qc, lb=lb, vd_lb=vd_lb, p=p, bk0=bk0, bk1=bk1):
                    rr = rot_next()
                    r.wait("tensor", gdc, gd_sel2)
                    r.wait("tensor", vd, vd_lb)
                    r.op("tensor", lambda e, lb=lb, rr=rr: e.matmul(
                        rr[0][:, 0:512], sel2[0:33, :], lb[0:33, 0:512],
                        start=True, stop=True), [(md, 1)])
                    lvb.release(md)
                    r.wait("vector", md, md.n)
                    lc = lvc.acquire(r, "vector")
                    r.op("vector", lambda e, rr=rr, lc=lc: e.tensor_copy(
                        lc[:, :], rr[0][:, 0:512]), [(vd, 1)])
                    rot_release(rr, vd)
                    r.raw("vector", lambda e: e.drain())
                    r.op("vector", lambda e, qc=qc, lc=lc, p=p, bk0=bk0: e.tensor_mul(
                        ansT[0:64, p * 2048 + qc * 512: p * 2048 + (qc + 1) * 512],
                        bk0.t[0:64, 0:512], lc[0:64, :]), [(vd, 1)])
                    bk0.release(vd)
                    r.op("vector", lambda e, qc=qc, lc=lc, p=p, bk1=bk1: e.tensor_mul(
                        ansT[64:128, p * 2048 + qc * 512: p * 2048 + (qc + 1) * 512],
                        bk1.t[0:64, 0:512], lc[64:128, :]), [(vd, 1)])
                    bk1.release(vd)
                    lvc.release(vd)
                evac_pend.append((step_box[0], rest))

            for hl in (0, 1):
                emit_scores(hl, 0)
            if half == 1:
                pump()
            for ki in range(nki):
                step_box[0] += 1
                if ki + 1 < nki:
                    emit_scores(0, ki + 1)
                emit_av(0, ki)
                if ki + 1 < nki:
                    emit_scores(1, ki + 1)
                emit_av(1, ki)
                if half == 1 and ki == NT - 1:
                    vhb.release(md)
                lqc_done = None
                if ki == 4 * (half * 2) + 3:
                    lqc_done = 0
                elif ki == min(NT - 1, 4 * (half * 2 + 1) + 3):
                    lqc_done = 1
                if lqc_done is not None:
                    emit_evac(lqc_done)
                pump()
                if (lqc_done is None and evac_pend
                        and step_box[0] - evac_pend[0][0] >= 2):
                    evac_pend.pop(0)[1]()

    evac_pend = []
    step_box = [0]
    jobs0, qt0, vt0 = make_qkv_jobs(0)
    gd_wout_box[0] = dma(
        "gpsimd", woutb[:, :].rearrange("p (hek dm) -> p hek dm", hek=8),
        d["wout_s"].rearrange("(hek p) dm -> p hek dm", p=128), gdw)
    for jb in jobs0:
        jb()
    prev = (qt0, vt0)
    pending = []

    def pump():
        if pending:
            pending.pop(0)()

    for p in range(NP):
        if p + 1 < NP:
            jobs, qtn, vtn = make_qkv_jobs(p + 1)
            pending.extend(jobs)
            nxt = (qtn, vtn)
        else:
            nxt = None
        emit_attn(p, prev[0], prev[1], pump)
        while pending:
            pending.pop(0)()
        prev = nxt

    while evac_pend:
        evac_pend.pop(0)[1]()
    r.wait("tensor", gdw, gd_wout_box[0])
    r.wait("tensor", vd, vd.n)
    for dm in range(ND):
        for qc in range(4):
            b = banks[(dm % 2) * 4 + qc]
            b.acquire(r)
            for hek in range(8):
                incs = [(md, 1)] if hek == 7 else []
                r.op("tensor", lambda e, dm=dm, qc=qc, hek=hek, b=b: e.matmul(
                    b.t[:, :], woutb[:, hek * 1024 + dm * 128: hek * 1024 + (dm + 1) * 128],
                    ansT[:, hek * 2048 + qc * 512: hek * 2048 + (qc + 1) * 512],
                    start=(hek == 0), stop=(hek == 7)), incs)
            r.wait("vector", md, md.n)
            r.wait("vector", sd, sd_acc_init)
            r.op("vector", lambda e, dm=dm, qc=qc, b=b: e.tensor_add(
                acc[:, dm * S + qc * 512: dm * S + (qc + 1) * 512],
                acc[:, dm * S + qc * 512: dm * S + (qc + 1) * 512], b.t[:, :]),
                [(vd, 1)])
            b.release(vd)
    r.flush(nc)
    es2.close()

    es3 = contextlib.ExitStack()
    sb3 = lambda name, shape, dtype: es3.enter_context(nc.sbuf_tensor(name, shape, dtype))
    gatb = sb3("gatbs", [128, 8], F32)
    b1c = sb3("b1cs", [128, 128], F32)
    b3c = sb3("b3cs", [128, 128], F32)
    sb1c = sb3("sb1cs", [128, 16], F32)
    sb3c = sb3("sb3cs", [128, 16], F32)
    selm = sb3("selm", [8, 1024], BF16)
    b2c = sb3("b2cs", [8, 1024], BF16)
    gate_ws = sb3("gate_wss", [128, 64], BF16)
    cw_T = sb3("cw_Ts", [8, 2048], BF16)
    gd_gate_w = dma("gpsimd", gate_ws[:, :], d["gate_ws"], gdg)
    for nm, t in [("gate_b", gatb), ("b1_s", b1c), ("b3_s", b3c),
                  ("sb1_s", sb1c), ("sb3_s", sb3c)]:
        dma("gpsimd", t[:, :], d[nm], gdf)
    dma("gpsimd", b2c[0:8, :], d["b2_s"], gdf)
    dma("gpsimd", selm[0:8, :], d_sel, gdf)
    gd_fconsts = gdf.n
    r.wait("tensor", gdg, gd_gate_w)
    r.wait("vector", gdf, gd_fconsts)
    lg_sb = sb3("lg_sb", [8, 512], F32)
    lg_tm = sb3("lg_tm", [128, 128], F32)
    em = sb3("em", [128, 128], F32)
    sct = sb3("sct", [128, 128], F32)
    mk1 = sb3("mk1", [128, 128], F32)
    msk = sb3("msk", [128, 128], F32)
    mk2 = sb3("mk2", [128, 128], F32)
    cwa = sb3("cwa", [128, 128], F32)
    gs1 = sb3("gs1", [128, 16], F32)
    gs2 = sb3("gs2", [128, 16], F32)
    gm1 = sb3("gm1", [128, 16], F32)
    gm2 = sb3("gm2", [128, 16], F32)
    a_sb = sb3("a_sb", [128, 8 * 2048], BF16)
    cwb = sb3("cwb", [128, 2048], BF16)
    pa = [sb3(f"pa{i}", [128, 512], F32) for i in range(2)]
    pbx = [sb3(f"pbx{i}", [128, 512], F32) for i in range(2)]
    mm_ = [sb3(f"mm{i}", [128, 512], F32) for i in range(2)]
    lp_ = [sb3(f"lp{i}", [128, 512], F32) for i in range(2)]
    aa_s = sb3("aa_s", [128, 512], BF16)
    w13b = Buf(nc, es3, "w13b", 3, [128, 4096], BF16)
    w2b = Buf(nc, es3, "w2b", 1, [128, 8192], BF16)

    b6 = banks[6]
    b6.acquire(r)
    for qc in range(4):
        b = banks[4 + qc % 2]
        b.acquire(r)
        for dk in range(ND):
            incs = [(md, 1)] if dk == ND - 1 else []
            r.op("tensor", lambda e, dk=dk, qc=qc, b=b: e.matmul(
                b.t[0:8, :], gate_ws[:, dk * 8:(dk + 1) * 8],
                xnf[:, dk * S + qc * 512: dk * S + qc * 512 + 512],
                start=(dk == 0), stop=(dk == ND - 1)), incs)
        r.wait("vector", md, md.n)
        r.op("vector", lambda e, b=b: e.tensor_copy(
            lg_sb[0:8, 0:512], b.t[0:8, :]), [(vd, 1)])
        b.release(vd)
        r.wait("tensor", vd, vd.n)
        for k in range(4):
            ti = qc * 4 + k
            incs = [(md, 1)] if k == 3 else []
            r.op("tensor", lambda e, ti=ti, k=k: e.transpose(
                b6.t[:, ti * 8:(ti + 1) * 8], lg_sb[0:8, k * 128:(k + 1) * 128],
                identf[0:8, 0:8]), incs)
        r.wait("vector", md, md.n)
    r.op("vector", lambda e: e.tensor_copy(lg_tm[:, 0:128], b6.t[:, 0:128]),
         [(vd, 1)])
    b6.release(vd)
    r.wait("scalar", vd, vd.n)
    r.op("scalar", lambda e: e.activation(em[:, :], lg_tm[:, :], AF.Exp), [(sd, 1)])
    r.wait("vector", sd, sd.n)
    for ti in range(NT):
        r.op("vector", lambda e, ti=ti: e.reduce_sum(
            gs1[:, ti:ti + 1], em[:, ti * 8:(ti + 1) * 8], axis=AX))
    r.raw("vector", lambda e: e.drain())
    r.op("vector", lambda e: e.reciprocal(gs2[:, :], gs1[:, :]))
    r.raw("vector", lambda e: e.drain())
    for ti in range(NT):
        r.op("vector", lambda e, ti=ti: e.tensor_scalar(
            sct[:, ti * 8:(ti + 1) * 8], em[:, ti * 8:(ti + 1) * 8],
            gs2[:, ti:ti + 1], None, ALU.mult))
    r.raw("vector", lambda e: e.drain())
    bia = lg_tm
    for ti in range(NT):
        r.op("vector", lambda e, ti=ti: e.tensor_add(
            bia[:, ti * 8:(ti + 1) * 8], sct[:, ti * 8:(ti + 1) * 8], gatb[:, :]))
    r.raw("vector", lambda e: e.drain())
    for ti in range(NT):
        r.op("vector", lambda e, ti=ti: e.reduce_max(
            gm1[:, ti:ti + 1], bia[:, ti * 8:(ti + 1) * 8], axis=AX))
    r.raw("vector", lambda e: e.drain())
    for ti in range(NT):
        r.op("vector", lambda e, ti=ti: e.tensor_scalar(
            mk1[:, ti * 8:(ti + 1) * 8], bia[:, ti * 8:(ti + 1) * 8],
            gm1[:, ti:ti + 1], None, ALU.is_ge))
    r.raw("vector", lambda e: e.drain())
    for ti in range(NT):
        r.op("vector", lambda e, ti=ti: e.scalar_tensor_tensor(
            msk[:, ti * 8:(ti + 1) * 8], mk1[:, ti * 8:(ti + 1) * 8], -1e9,
            bia[:, ti * 8:(ti + 1) * 8], ALU.mult, ALU.add))
    r.raw("vector", lambda e: e.drain())
    for ti in range(NT):
        r.op("vector", lambda e, ti=ti: e.reduce_max(
            gm2[:, ti:ti + 1], msk[:, ti * 8:(ti + 1) * 8], axis=AX))
    r.raw("vector", lambda e: e.drain())
    for ti in range(NT):
        r.op("vector", lambda e, ti=ti: e.tensor_scalar(
            mk2[:, ti * 8:(ti + 1) * 8], msk[:, ti * 8:(ti + 1) * 8],
            gm2[:, ti:ti + 1], None, ALU.is_ge))
    r.raw("vector", lambda e: e.drain())
    for ti in range(NT):
        r.op("vector", lambda e, ti=ti: e.tensor_add(
            mk1[:, ti * 8:(ti + 1) * 8], mk1[:, ti * 8:(ti + 1) * 8],
            mk2[:, ti * 8:(ti + 1) * 8]))
    r.raw("vector", lambda e: e.drain())
    for ti in range(NT):
        incs = [(vd, 1)] if ti == NT - 1 else []
        r.op("vector", lambda e, ti=ti: e.tensor_mul(
            cwa[:, ti * 8:(ti + 1) * 8], sct[:, ti * 8:(ti + 1) * 8],
            mk1[:, ti * 8:(ti + 1) * 8]), incs)
    vd_cwt_box = [0]

    def emit_cwT():
        r.wait("tensor", vd, vd.n)
        for grp in range(4):
            b = banks[4 + grp % 2]
            b.acquire(r)
            for k in range(4):
                ti = grp * 4 + k
                incs = [(md, 1)] if k == 3 else []
                r.op("tensor", lambda e, ti=ti, k=k, b=b: e.transpose(
                    b.t[0:8, k * 128:(k + 1) * 128], cwa[:, ti * 8:(ti + 1) * 8],
                    identf[:, :]), incs)
            r.wait("vector", md, md.n)
            r.op("vector", lambda e, grp=grp, b=b: e.tensor_copy(
                cw_T[0:8, grp * 512:(grp + 1) * 512], b.t[0:8, 0:512]), [(vd, 1)])
            b.release(vd)
        vd_cwt_box[0] = vd.n

    bank_sets = [(0, 1, 2, 3), (4, 5, 0, 1), (2, 3, 4, 5)]
    gidx = [0]
    last_ap_vd = [0, 0, 0, 0]
    pre_ap_vd = [0, 0, 0, 0]
    pend = []
    sdml_hist = []

    def emit_hgroup(e_idx, j, qc, shared):
        i = gidx[0]
        gidx[0] += 1
        par = i % 2
        st = bank_sets[i % 3]
        bt = [banks[st[0]], banks[st[1]], banks[st[2]], banks[st[3]]]
        wsl, w13_ats, w13_sl = cur_w13[0]
        b1col = (sb1c if shared else b1c)[:, (0 if shared else e_idx) * 16 + j: (0 if shared else e_idx) * 16 + j + 1]
        b3col = (sb3c if shared else b3c)[:, (0 if shared else e_idx) * 16 + j: (0 if shared else e_idx) * 16 + j + 1]
        b1colB = (sb1c if shared else b1c)[:, (0 if shared else e_idx) * 16 + 8 + j: (0 if shared else e_idx) * 16 + 8 + j + 1]
        b3colB = (sb3c if shared else b3c)[:, (0 if shared else e_idx) * 16 + 8 + j: (0 if shared else e_idx) * 16 + 8 + j + 1]
        for src in range(4):
            b = bt[src]
            wsem, wat = w13_ats[src]
            r.wait("tensor", wsem, wat)
            b.acquire(r)
            for dk in range(ND):
                incs = [(md, 1)] if src == 3 and dk == ND - 1 else []
                r.op("tensor", lambda e, src=src, dk=dk, qc=qc, b=b, wsl=wsl: e.matmul(
                    b.t[:, :], wsl[:, (src * 8 + dk) * 128:(src * 8 + dk + 1) * 128],
                    xnf[:, dk * S + qc * 512: dk * S + qc * 512 + 512],
                    start=(dk == 0), stop=(dk == ND - 1)), incs)
        md_h = md.n
        if qc == 3:
            w13b.release(md)
        r.wait("vector", md, md_h)
        if len(sdml_hist) >= 2:
            r.wait("vector", sd, sdml_hist[-2])
        r.op("vector", lambda e, par=par, bt=bt, b3col=b3col: e.tensor_scalar(
            pa[par][:, :], bt[1].t[:, :], b3col, None, ALU.add), [(vd, 1)])
        bt[1].release(vd)
        r.op("vector", lambda e, par=par, bt=bt, b3colB=b3colB: e.tensor_scalar(
            pbx[par][:, :], bt[3].t[:, :], b3colB, None, ALU.add), [(vd, 1)])
        bt[3].release(vd)
        r.raw("vector", lambda e: e.drain())
        r.op("vector", lambda e, par=par, bt=bt, b1col=b1col: e.scalar_tensor_tensor(
            pa[par][:, :], bt[0].t[:, :], b1col, pa[par][:, :], ALU.add, ALU.mult),
            [(vd, 1)])
        bt[0].release(vd)
        r.op("vector", lambda e, par=par, bt=bt, b1colB=b1colB: e.scalar_tensor_tensor(
            pbx[par][:, :], bt[2].t[:, :], b1colB, pbx[par][:, :], ALU.add, ALU.mult),
            [(vd, 1)])
        bt[2].release(vd)
        r.raw("vector", lambda e: e.drain())
        if pend:
            pend.pop(0)(0)
        r.op("vector", lambda e, par=par: e.tensor_scalar(
            pa[par][:, :], pa[par][:, :], 7.0, None, ALU.min), [(vd, 1)])
        vd_g = vd.n
        r.raw("vector", lambda e: e.drain())
        if pend:
            pend.pop(0)(0)
        r.op("vector", lambda e, par=par: e.tensor_scalar(
            pbx[par][:, :], pbx[par][:, :], 7.0, -7.0, ALU.min, ALU.max), [(vd, 1)])
        vd_u = vd.n
        r.wait("scalar", vd, vd_g)
        r.op("scalar", lambda e, par=par: e.activation(
            mm_[par][:, :], pa[par][:, :], AF.Silu, scale=1.702))
        r.wait("scalar", vd, vd_u)
        r.op("scalar", lambda e, par=par: e.activation(
            lp_[par][:, :], pbx[par][:, :], AF.Identity, scale=1.0 / 1.702,
            bias=cInv[:, 0:1]), [(sd, 1)])
        sd_ml = sd.n
        sdml_hist.append(sd_ml)
        dst = a_sb[:, j * 2048 + qc * 512: j * 2048 + (qc + 1) * 512]

        def fin_aa(_=0, par=par, dst=dst, sd_ml=sd_ml, shared=shared, qc=qc, j=j):
            r.wait("vector", sd, sd_ml)
            if shared:
                r.op("vector", lambda e: e.tensor_mul(dst, mm_[par][:, :], lp_[par][:, :]),
                     [(vd, 1)])
                if j == 7:
                    last_ap_vd[qc] = vd.n
                elif j == 6:
                    pre_ap_vd[qc] = vd.n
            else:
                r.op("vector", lambda e: e.tensor_mul(aa_s[:, :], mm_[par][:, :], lp_[par][:, :]))

        def fin_ap(_=0, dst=dst, shared=shared, qc=qc, j=j):
            if not shared:
                r.op("vector", lambda e: e.tensor_mul(
                    dst, aa_s[:, :], cwb[:, qc * 512:(qc + 1) * 512]), [(vd, 1)])
                if j == 7:
                    last_ap_vd[qc] = vd.n
                elif j == 6:
                    pre_ap_vd[qc] = vd.n

        pend.append(fin_aa)
        pend.append(fin_ap)

    cur_w13 = [None]
    w13cnt = [0]

    def emit_expert(e_idx):
        shared = e_idx == EXP
        if not shared:
            r.wait("tensor", vd, vd_cwt_box[0])
            for qc in range(4):
                b = banks[6 + qc % 2]
                b.acquire(r)
                incs = [(md, 1)]
                r.op("tensor", lambda e, e_idx=e_idx, qc=qc, b=b: e.matmul(
                    b.t[:, :], selm[0:8, e_idx * 128:(e_idx + 1) * 128],
                    cw_T[0:8, qc * 512:(qc + 1) * 512], start=True, stop=True), incs)
                r.wait("vector", md, md.n)
                r.op("vector", lambda e, qc=qc, b=b: e.tensor_copy(
                    cwb[:, qc * 512:(qc + 1) * 512], b.t[:, :]), [(vd, 1)])
                b.release(vd)
            r.raw("vector", lambda e: e.drain())
        slabs = []

        def issue_w13(j):
            wsl = w13b.acquire(r, "gpsimd")
            i = w13cnt[0]
            w13cnt[0] += 1
            sl = i % 3
            src_ap = d["sw13_s"][j] if shared else d["w13_s"][e_idx, j]
            at = dma("gpsimd", wsl[:, :], src_ap, w13s[sl * 4])
            slabs.append((wsl, [(w13s[sl * 4], at)] * 4, sl))

        issue_w13(0)
        issue_w13(1)
        issue_w13(2)
        wv = w2b.acquire(r, "gpsimd")
        gd_w2 = dma("gpsimd", wv[:, :], d["sw2_s"] if shared else d["w2_s"][e_idx], w2s)
        for j in range(8):
            if j >= 3:
                issue_w13(j)
            cur_w13[0] = slabs[j]
            for qc in range(4):
                emit_hgroup(e_idx, j, qc, shared)
        while pend:
            pend.pop(0)(0)
            r.raw("vector", lambda e: e.drain())
        r.wait("tensor", w2s, gd_w2)

        def w2_mms(qc, dm, b, fks):
            first = fks[0] == 0
            if e_idx == 0 and first:
                r.op("tensor", lambda e, dm=dm, qc=qc, b=b: e.matmul(
                    b.t[:, :], b2c[0:8, dm * 128:(dm + 1) * 128],
                    cw_T[0:8, qc * 512:(qc + 1) * 512], start=True, stop=False))
                first = False
            for fk in fks:
                incs = [(md, 1)] if fk == 7 else []
                r.op("tensor", lambda e, dm=dm, qc=qc, fk=fk, b=b, wv=wv,
                     st=(first and fk == fks[0]): e.matmul(
                    b.t[:, :], wv[:, fk * 1024 + dm * 128: fk * 1024 + (dm + 1) * 128],
                    a_sb[:, fk * 2048 + qc * 512: fk * 2048 + (qc + 1) * 512],
                    start=st, stop=(fk == 7)), incs)

        def w2_evac(qc, dm, b):
            r.wait("vector", md, md.n)
            r.op("vector", lambda e, dm=dm, qc=qc, b=b: e.tensor_add(
                acc[:, dm * S + qc * 512: dm * S + (qc + 1) * 512],
                acc[:, dm * S + qc * 512: dm * S + (qc + 1) * 512], b.t[:, :]),
                [(vd, 1)])
            b.release(vd)

        s1 = bank_sets[(gidx[0] - 2) % 3]
        s2 = bank_sets[(gidx[0] - 1) % 3]
        extra = [bi for bi in s1 if bi not in s2][:2]
        pre = [banks[6], banks[7], banks[extra[0]], banks[extra[1]]]
        r.wait("tensor", vd, pre_ap_vd[0])
        for dm in range(4):
            pre[dm].acquire(r)
            w2_mms(0, dm, pre[dm], list(range(7)))
        r.wait("tensor", vd, last_ap_vd[0])
        for dm in range(4):
            w2_mms(0, dm, pre[dm], [7])
            w2_evac(0, dm, pre[dm])
        for qc in range(4):
            r.wait("tensor", vd, last_ap_vd[qc])
            for dm in range(4 if qc == 0 else 0, ND):
                b = banks[6 + (qc * 8 + dm) % 2]
                b.acquire(r)
                w2_mms(qc, dm, b, list(range(8)))
                if dm == ND - 1 and qc == 3:
                    w2b.release(md)
                w2_evac(qc, dm, b)

    emit_expert(EXP)
    r.wait("tensor", gdf, gd_fconsts)
    emit_cwT()
    for e_idx in range(EXP):
        emit_expert(e_idx)
    vd_ffn_done = vd.n
    r.flush(nc)
    es3.close()

    es4 = contextlib.ExitStack()
    otb = Buf(nc, es4, "otb", 2, [128, 1024], F32)
    r.wait("tensor", vd, vd_ffn_done)
    for ti in range(NT):
        ot = otb.acquire(r, "vector")
        bA = banks[(2 * ti) % 8]
        bB = banks[(2 * ti + 1) % 8]
        bA.acquire(r)
        bB.acquire(r)
        for dk in range(ND):
            b = bA if dk < 4 else bB
            incs = [(md, 1)] if dk % 4 == 3 else []
            r.op("tensor", lambda e, dk=dk, ti=ti, b=b: e.transpose(
                b.t[:, (dk % 4) * 128:(dk % 4 + 1) * 128],
                acc[:, dk * S + ti * 128: dk * S + (ti + 1) * 128],
                identf[:, :]), incs)
        r.wait("vector", md, md.n)
        r.op("vector", lambda e, ot=ot, bA=bA: e.tensor_copy(
            ot[:, 0:512], bA.t[:, :]), [(vd, 1)])
        bA.release(vd)
        r.op("vector", lambda e, ot=ot, bB=bB: e.tensor_copy(
            ot[:, 512:1024], bB.t[:, :]), [(vd, 1)])
        bB.release(vd)
        r.wait("sync", vd, vd.n)
        dma("sync", out_hbm[ti * 128:(ti + 1) * 128, :], ot[:, :], ods[ti % 2])
        otb.release(ods[ti % 2])
    r.wait("sync", ods[0], ods[0].n)
    r.wait("sync", ods[1], ods[1].n)
    r.flush(nc)
    es4.close()
    esP.close()
    es0.close()
    return nc


def kernel(**inputs):
    x_all = np.asarray(inputs["L0_attn_x"], dtype=np.float32)
    lay = host_layouts(inputs)
    nc = build_nc()
    in_maps = []
    for c in range(8):
        m = {"x": np.ascontiguousarray(x_all[c])}
        m.update(lay)
        in_maps.append(m)
    res = run_bass_kernel_spmd(nc, in_maps, core_ids=list(range(8)))
    return np.stack([np.asarray(r["out"], dtype=np.float32) for r in res.results], axis=0)


if __name__ == "__main__":
    import reference as _ref
    inp = _ref.setup_inputs()
    out = kernel(**{k: np.asarray(v) for k, v in inp.items()})
    exp = np.asarray(_ref.reference(**inp))
    rel = np.linalg.norm(out - exp) / np.linalg.norm(exp)
    print("rel:", rel)
